# version 04
# baseline (speedup 1.0000x reference)
import numpy as np
from contextlib import ExitStack

import concourse.bass as bass
import concourse.mybir as mybir
from concourse.bass_utils import run_bass_kernel_spmd

F32 = mybir.dt.float32
BF16 = mybir.dt.bfloat16
AF = mybir.ActivationFunctionType
ALU = mybir.AluOpType
AX = mybir.AxisListType

S = 4
N = 2048
D = 1024
H = 16
E = 8
DFF = 4096
NT = N // 128
DT = D // 128
FT = DFF // 128

ENGINES = ("sync", "tensor", "vector", "scalar", "gpsimd")


class Ctr:
    def __init__(self, nc, es, name):
        self.h = es.enter_context(nc.semaphore(name))
        self.name = name
        self.n = 0

    def bump(self, k=1):
        self.n += k
        return self.n


class EM:
    def __init__(self, nc):
        self.nc = nc
        self.q = {e: [] for e in ENGINES}
        self.lastwait = {}

    def w(self, eng, ctr, val):
        if val is None or val <= 0:
            return
        assert val <= ctr.n, f"wait on future count {ctr.name} {val} > {ctr.n}"
        key = (eng, ctr.name)
        if self.lastwait.get(key, 0) >= val:
            return
        self.lastwait[key] = val
        self.q[eng].append(lambda Eng, h=ctr.h, v=val: Eng.wait_ge(h, v))

    def op(self, eng, fn, inc=None, k=1):
        if inc is not None:
            val = inc.bump(k)

            def emit(Eng, fn=fn, h=inc.h, k=k):
                fn(Eng).then_inc(h, k)
            self.q[eng].append(emit)
            return val
        self.q[eng].append(lambda Eng, fn=fn: fn(Eng))
        return None

    def drain(self, eng, inc=None):
        return self.op(eng, lambda Eng: Eng.drain(), inc=inc)

    def flush(self):
        qs = self.q
        self.q = {e: [] for e in ENGINES}
        if not any(qs.values()):
            return
        with self.nc.Block() as blk:
            for e in ENGINES:
                fns = qs[e]
                if fns:
                    getattr(blk, e)(lambda Eng, fns=fns: [f(Eng) for f in fns])


def build():
    nc = bass.Bass("TRN2", target_bir_lowering=False)

    xin = nc.dram_tensor("x", [S, N, D], F32, kind="ExternalInput")
    wq = nc.dram_tensor("wq", [D, D], F32, kind="ExternalInput")
    wk = nc.dram_tensor("wk", [D, D], F32, kind="ExternalInput")
    wv = nc.dram_tensor("wv", [D, D], F32, kind="ExternalInput")
    wo = nc.dram_tensor("wo", [D, D], F32, kind="ExternalInput")
    bo = nc.dram_tensor("bo", [1, D], F32, kind="ExternalInput")
    wdec = nc.dram_tensor("wdec", [1, H], F32, kind="ExternalInput")
    vdec = nc.dram_tensor("vdec", [1, H], F32, kind="ExternalInput")
    gw = nc.dram_tensor("gw", [E, D], F32, kind="ExternalInput")
    gb = nc.dram_tensor("gb", [1, E], F32, kind="ExternalInput")
    ebias = nc.dram_tensor("ebias", [1, E], F32, kind="ExternalInput")
    w1 = nc.dram_tensor("w1", [E, D, DFF], F32, kind="ExternalInput")
    b1r = nc.dram_tensor("b1r", [128, FT * E], F32, kind="ExternalInput")
    w2 = nc.dram_tensor("w2", [E, DFF, D], F32, kind="ExternalInput")
    b2 = nc.dram_tensor("b2", [E, D], F32, kind="ExternalInput")
    eye = nc.dram_tensor("eye", [128, 128], F32, kind="ExternalInput")
    onesr = nc.dram_tensor("onesr", [1, 128], F32, kind="ExternalInput")
    yout = nc.dram_tensor("y", [S, N, D], F32, kind="ExternalOutput")

    w1b = nc.dram_tensor("w1b", [E, D, DFF], BF16)
    w2b = nc.dram_tensor("w2b", [E, DFF, D], BF16)
    qts = nc.dram_tensor("qts", [S, D, N], BF16)
    kts = nc.dram_tensor("kts", [S, D, N], BF16)
    vs = nc.dram_tensor("vs", [S, N, D], BF16)
    cts = nc.dram_tensor("cts", [S, D, N], BF16)

    def dap(t, off, dims):
        return bass.AP(t, off, dims)

    es = ExitStack()
    em = EM(nc)
    c_ld = Ctr(nc, es, "c_ld")
    c_st = Ctr(nc, es, "c_st")
    c_t = Ctr(nc, es, "c_t")
    c_v = Ctr(nc, es, "c_v")
    c_s = Ctr(nc, es, "c_s")
    c_g = Ctr(nc, es, "c_g")

    eye_f = es.enter_context(nc.sbuf_tensor("eye_f", [128, 128], F32))
    eye_b = es.enter_context(nc.sbuf_tensor("eye_b", [128, 128], BF16))
    ones_f = es.enter_context(nc.sbuf_tensor("ones_f", [1, 128], F32))
    rowbuf = es.enter_context(nc.sbuf_tensor("rowbuf", [1, 2 * H], F32))
    grow = es.enter_context(nc.sbuf_tensor("grow", [1, 2 * E], F32))
    bobc = es.enter_context(nc.sbuf_tensor("bobc", [128, D], F32))
    gbb = es.enter_context(nc.sbuf_tensor("gbb", [128, E], F32))
    wcol = es.enter_context(nc.sbuf_tensor("wcol", [128, H], F32))
    nvcol = es.enter_context(nc.sbuf_tensor("nvcol", [128, H], F32))
    c8col = es.enter_context(nc.sbuf_tensor("c8col", [128, H], F32))
    b1s = es.enter_context(nc.sbuf_tensor("b1s", [128, FT * E], F32))
    b2s = es.enter_context(nc.sbuf_tensor("b2s", [E, D], F32))
    epsc = es.enter_context(nc.sbuf_tensor("epsc", [128, 1], F32))

    psA = es.enter_context(nc.psum_tensor("psA", [128, 512], F32))
    psB = es.enter_context(nc.psum_tensor("psB", [128, 512], F32))
    psC = es.enter_context(nc.psum_tensor("psC", [128, 512], F32))
    psD = es.enter_context(nc.psum_tensor("psD", [128, 512], F32))
    psE2 = es.enter_context(nc.psum_tensor("psE2", [128, 512], F32))
    psT = es.enter_context(nc.psum_tensor("psT", [128, 1024], BF16))

    em.op("sync", lambda En: En.dma_start(out=eye_f[:, :], in_=eye.ap()), inc=c_ld, k=16)
    em.op("sync", lambda En: En.dma_start(out=ones_f[:, :], in_=onesr.ap()), inc=c_ld, k=16)
    em.op("sync", lambda En: En.dma_start(out=rowbuf[:, 0:H], in_=wdec.ap()), inc=c_ld, k=16)
    em.op("sync", lambda En: En.dma_start(out=rowbuf[:, H:2 * H], in_=vdec.ap()), inc=c_ld, k=16)
    em.op("sync", lambda En: En.dma_start(out=grow[:, 0:E], in_=gb.ap()), inc=c_ld, k=16)
    em.op("sync", lambda En: En.dma_start(out=grow[:, E:2 * E], in_=ebias.ap()), inc=c_ld, k=16)
    em.op("sync", lambda En: En.dma_start(out=b1s[:, :], in_=b1r.ap()), inc=c_ld, k=16)
    ld0 = em.op("sync", lambda En: En.dma_start(out=b2s[:, :], in_=b2.ap()), inc=c_ld, k=16)

    em.w("vector", c_ld, ld0)
    em.op("vector", lambda En: En.memset(epsc[:, :], 1e-5), inc=c_v)
    em.op("vector", lambda En: En.tensor_copy(eye_b[:, :], eye_f[:, :]), inc=c_v)
    em.op("vector", lambda En: En.tensor_tensor(grow[:, 0:E], grow[:, 0:E], grow[:, E:2 * E], op=ALU.add))
    v_grow = em.drain("vector", inc=c_v)

    em.w("tensor", c_ld, ld0)
    em.w("tensor", c_v, v_grow)
    t_b1 = em.op("tensor", lambda En: En.matmul(psA[:, 0:2 * H], ones_f[:, :], rowbuf[:, :], start=True, stop=True), inc=c_t)
    t_b2 = em.op("tensor", lambda En: En.matmul(psB[:, 0:E], ones_f[:, :], grow[:, 0:E], start=True, stop=True), inc=c_t)
    em.w("vector", c_t, t_b2)
    em.op("vector", lambda En: En.tensor_copy(wcol[:, :], psA[:, 0:H]), inc=c_v)
    em.op("vector", lambda En: En.tensor_scalar_mul(nvcol[:, :], psA[:, H:2 * H], -1.0), inc=c_v)
    em.op("vector", lambda En: En.tensor_copy(gbb[:, :], psB[:, 0:E]), inc=c_v)
    em.w("scalar", c_t, t_b1)
    em.op("scalar", lambda En: En.activation(c8col[:, :], psA[:, H:2 * H], AF.Exp))
    s_ev = em.drain("scalar", inc=c_s)
    em.w("vector", c_s, s_ev)
    em.op("vector", lambda En: En.tensor_scalar(out=c8col[:, :], in0=c8col[:, :], scalar1=1.0, scalar2=0.125, op0=ALU.add, op1=ALU.mult))
    v_c8 = em.drain("vector", inc=c_v)

    wo_b = es.enter_context(nc.sbuf_tensor("wo_b", [128, DT * D], BF16))
    es_p1 = ExitStack()
    wq_b = es_p1.enter_context(nc.sbuf_tensor("wq_b", [128, DT * D], BF16))
    wk_b = es_p1.enter_context(nc.sbuf_tensor("wk_b", [128, DT * D], BF16))
    wv_b = es_p1.enter_context(nc.sbuf_tensor("wv_b", [128, DT * D], BF16))

    with nc.sbuf_tensor("pp_f", [128, 2 * 8192], F32) as pp_f, \
         nc.sbuf_tensor("pp_b", [128, 2 * 8192], BF16) as pp_b, \
         nc.sbuf_tensor("borow", [1, D], F32) as borow:
        ldbo = em.op("sync", lambda En: En.dma_start(out=borow[:, :], in_=bo.ap()), inc=c_ld, k=16)
        em.w("tensor", c_ld, ldbo)
        em.w("tensor", c_v, v_c8)
        em.op("tensor", lambda En: En.matmul(psC[:, 0:512], ones_f[:, :], borow[:, 0:512], start=True, stop=True), inc=c_t)
        t_b3 = em.op("tensor", lambda En: En.matmul(psD[:, 0:512], ones_f[:, :], borow[:, 512:1024], start=True, stop=True), inc=c_t)
        em.w("vector", c_t, t_b3)
        em.op("vector", lambda En: En.tensor_copy(bobc[:, 0:512], psC[:, 0:512]), inc=c_v)
        em.op("vector", lambda En: En.tensor_copy(bobc[:, 512:1024], psD[:, 0:512]), inc=c_v)
        prev_cast = 0
        for (src, dstb) in ((wq, wq_b), (wk, wk_b), (wv, wv_b), (wo, wo_b)):
            em.w("sync", c_v, prev_cast)
            lv = em.op("sync", lambda En, src=src: En.dma_start(
                out=pp_f[:, 0:8192],
                in_=dap(src, 0, [[D, 128], [128 * D, DT], [1, D]])), inc=c_ld, k=16)
            em.w("vector", c_ld, lv)
            prev_cast = em.op("vector", lambda En, dstb=dstb: En.tensor_copy(dstb[:, :], pp_f[:, 0:8192]), inc=c_v)

        cast_jobs = []
        for i in range(E * D // 128):
            cast_jobs.append((w1, w1b, i * 128 * DFF, DFF))
        for i in range(E * DFF // 128):
            cast_jobs.append((w2, w2b, i * 128 * D, D))
        half = 8192
        slot_cast = [prev_cast, prev_cast]
        slot_store = [0, 0]
        for j, (src, dst, off, width) in enumerate(cast_jobs):
            sl = j % 2
            em.w("sync", c_v, slot_cast[sl])
            lv = em.op("sync", lambda En, src=src, off=off, width=width, sl=sl: En.dma_start(
                out=pp_f[:, sl * half: sl * half + width],
                in_=dap(src, off, [[width, 128], [1, width]])), inc=c_ld, k=16)
            em.w("vector", c_ld, lv)
            em.w("vector", c_st, slot_store[sl])
            cv = em.op("vector", lambda En, sl=sl, width=width: En.tensor_copy(
                pp_b[:, sl * half: sl * half + width], pp_f[:, sl * half: sl * half + width]), inc=c_v)
            slot_cast[sl] = cv
            em.w("gpsimd", c_v, cv)
            slot_store[sl] = em.op("gpsimd", lambda En, dst=dst, off=off, width=width, sl=sl: En.dma_start(
                out=dap(dst, off, [[width, 128], [1, width]]),
                in_=pp_b[:, sl * half: sl * half + width]), inc=c_st, k=16)
        em.flush()

    prepass_v = c_v.n
    prepass_s = c_s.n
    prepass_ld = c_ld.n
    prepass_st = c_st.n

    p1_seq_st = []
    proj_last_t = 0
    p1_last_evac = 0
    stg_free = [0, 0, 0, 0]
    evacs = []
    with nc.sbuf_tensor("xt_b", [128, DT * N], BF16) as xt_b, \
         nc.sbuf_tensor("stg", [128, 4 * 512], BF16) as stg:
        for seq in range(S):
            with nc.sbuf_tensor(f"xf{seq}", [128, NT * 1024], F32) as xf:
                if seq > 0:
                    em.w("sync", c_t, proj_last_t)
                lx = em.op("sync", lambda En, seq=seq: En.dma_start(
                    out=xf[:, :],
                    in_=dap(xin, seq * N * D, [[D, 128], [128 * D, NT], [1, D]])), inc=c_ld, k=16)
                tr_evac = []
                for idx in range(NT * 2):
                    tt, hb = divmod(idx, 2)
                    bank = psA if idx % 2 == 0 else psB
                    em.w("tensor", c_ld, lx)
                    em.w("tensor", c_v, prepass_v)
                    if idx >= 2:
                        em.w("tensor", c_v, tr_evac[idx - 2])
                    elif seq > 0:
                        em.w("tensor", c_v, p1_last_evac)
                    for q in range(4):
                        dt = hb * 4 + q
                        t_tr = em.op("tensor", lambda En, bank=bank, q=q, tt=tt, dt=dt: En.transpose(
                            bank[:, q * 128:(q + 1) * 128],
                            xf[:, tt * 1024 + dt * 128: tt * 1024 + (dt + 1) * 128],
                            eye_f[:, :]), inc=c_t)
                    em.w("vector", c_t, t_tr)
                    ev = em.op("vector", lambda En, bank=bank, tt=tt, hb=hb: En.tensor_copy(
                        dap(xt_b, hb * 4 * N + tt * 128, [[DT * N, 128], [N, 4], [1, 128]]),
                        bank[:, 0:512]), inc=c_v)
                    tr_evac.append(ev)
                x_tr_done = tr_evac[-1]

            si = 0
            for (wgt, dst, transposed) in ((wq_b, qts, True), (wk_b, kts, True), (wv_b, vs, False)):
                for a in range(DT if transposed else NT):
                    for g in range(4 if transposed else 2):
                        bank = psC if si % 2 == 0 else psD
                        em.w("tensor", c_v, x_tr_done)
                        if len(evacs) >= 2:
                            em.w("tensor", c_v, evacs[-2])
                        for dk in range(DT):
                            if transposed:
                                mm = em.op("tensor", lambda En, bank=bank, wgt=wgt, dk=dk, a=a, g=g: En.matmul(
                                    bank[:, :], wgt[:, dk * D + a * 128: dk * D + (a + 1) * 128],
                                    xt_b[:, dk * N + g * 512: dk * N + (g + 1) * 512],
                                    start=(dk == 0), stop=(dk == DT - 1)), inc=c_t)
                            else:
                                mm = em.op("tensor", lambda En, bank=bank, wgt=wgt, dk=dk, a=a, g=g: En.matmul(
                                    bank[:, :], xt_b[:, dk * N + a * 128: dk * N + (a + 1) * 128],
                                    wgt[:, dk * D + g * 512: dk * D + (g + 1) * 512],
                                    start=(dk == 0), stop=(dk == DT - 1)), inc=c_t)
                        proj_last_t = mm
                        sslot = si % 4
                        em.w("vector", c_t, mm)
                        em.w("vector", c_st, stg_free[sslot])
                        ev = em.op("vector", lambda En, bank=bank, sslot=sslot: En.tensor_copy(
                            stg[:, sslot * 512:(sslot + 1) * 512], bank[:, :]), inc=c_v)
                        evacs.append(ev)
                        em.w("gpsimd", c_v, ev)
                        if transposed:
                            off = seq * D * N + a * 128 * N + g * 512
                            dims = [[N, 128], [1, 512]]
                        else:
                            off = seq * N * D + a * 128 * D + g * 512
                            dims = [[D, 128], [1, 512]]
                        stv = em.op("gpsimd", lambda En, dst=dst, off=off, dims=dims, sslot=sslot: En.dma_start(
                            out=dap(dst, off, dims), in_=stg[:, sslot * 512:(sslot + 1) * 512]), inc=c_st, k=16)
                        stg_free[sslot] = stv
                        si += 1
            p1_last_evac = evacs[-1]
            p1_seq_st.append(c_st.n)
            em.flush()
    p1_done_v = c_v.n
    es_p1.close()

    NB = 19
    with nc.sbuf_tensor("mstx", [128, 2816], F32) as mstx, \
         nc.sbuf_tensor("abl", [128, NB * 512], F32) as abl, \
         nc.sbuf_tensor("ktb", [64, 2 * N], BF16) as ktb, \
         nc.sbuf_tensor("qtb", [64, 2 * N], BF16) as qtb, \
         nc.sbuf_tensor("vxb", [128, 2 * NT * 65], BF16) as vxb, \
         nc.sbuf_tensor("pb", [128, 3 * 512], BF16) as pb, \
         nc.sbuf_tensor("pm", [128, 3 * 512], BF16) as pm, \
         nc.sbuf_tensor("sbs", [128, 2 * 512], F32) as sbs, \
         nc.sbuf_tensor("sbctx", [64, 512], F32) as sbctx, \
         nc.sbuf_tensor("rd", [1, 512], F32) as rd, \
         nc.sbuf_tensor("ctxn", [64, 2 * N], BF16) as ctxn:
        g_iota = em.op("gpsimd", lambda En: En.iota(
            mstx[:, :], [[1, 2816]], base=-384, channel_multiplier=-1,
            allow_small_or_imprecise_dtypes=True), inc=c_g)
        em.w("vector", c_g, g_iota)
        em.op("vector", lambda En: En.tensor_scalar_max(mstx[:, :], mstx[:, :], -30.0))
        v_mst = em.drain("vector", inc=c_v)
        v_vx1 = em.op("vector", lambda En: En.memset(
            dap(vxb, 64, [[2 * NT * 65, 128], [65, 2 * NT], [1, 1]]), 1.0), inc=c_v)

        qkv_ready = {}
        slot_free_t = [0, 0]
        ctxn_free = [0, 0]
        M = H * S

        def issue_qkv_load(m):
            h, seq = divmod(m, S)
            sl = m % 2
            em.w("sync", c_t, slot_free_t[sl])
            em.w("sync", c_v, v_vx1)
            em.w("sync", c_st, p1_seq_st[seq])
            em.op("sync", lambda En, seq=seq, h=h, sl=sl: En.dma_start(
                out=ktb[:, sl * N:(sl + 1) * N],
                in_=dap(kts, seq * D * N + h * 64 * N, [[N, 64], [1, N]])), inc=c_ld, k=16)
            em.op("sync", lambda En, seq=seq, h=h, sl=sl: En.dma_start(
                out=qtb[:, sl * N:(sl + 1) * N],
                in_=dap(qts, seq * D * N + h * 64 * N, [[N, 64], [1, N]])), inc=c_ld, k=16)
            qkv_ready[m] = em.op("sync", lambda En, seq=seq, h=h, sl=sl: En.dma_start(
                out=dap(vxb, sl * NT * 65, [[2 * NT * 65, 128], [65, NT], [1, 64]]),
                in_=dap(vs, seq * N * D + h * 64, [[D, 128], [128 * D, NT], [1, 64]])), inc=c_ld, k=16)

        issue_qkv_load(0)
        issue_qkv_load(1)

        blk_j = 0
        stt_done = {}
        exp_done = {}
        ctx_done = {}
        pm_free = {}
        ctx_bankfree_v = {0: p1_done_v, 1: p1_done_v}
        ctx_bankfree_s = {0: 0, 1: 0}
        prev_tbc = 0
        prev_vev = 0
        abl_last_user = 0
        first_p2 = True
        for h in range(H):
            em.w("scalar", c_v, v_mst)
            em.w("scalar", c_v, v_c8)
            em.w("scalar", c_v, abl_last_user)
            a_ready = 0
            for b in range(NB):
                u0 = 384 + 128 * (b - 3)
                em.op("scalar", lambda En, b=b, u0=u0, h=h: En.activation(
                    abl[:, b * 512:(b + 1) * 512], mstx[:, u0:u0 + 512], AF.Sigmoid,
                    bias=nvcol[:, h:h + 1], scale=wcol[:, h:h + 1]))
                sg = em.drain("scalar", inc=c_s)
                em.w("vector", c_s, sg)
                a_ready = em.op("vector", lambda En, b=b, h=h: En.tensor_scalar_mul(
                    abl[:, b * 512:(b + 1) * 512], abl[:, b * 512:(b + 1) * 512],
                    c8col[:, h:h + 1]), inc=c_v)

            for seq in range(S):
                m = h * S + seq
                sl = m % 2
                em.w("tensor", c_ld, qkv_ready[m])
                if first_p2:
                    em.w("tensor", c_v, p1_done_v)
                    first_p2 = False
                cb = m % 2
                last_t_this = 0
                v_ev = 0
                for qg in range(4):
                    cxid = qg % 2
                    cxbank = psC if cxid == 0 else psD
                    nkt = 4 * qg + 4
                    for kt in range(nkt):
                        beta = 4 * qg - kt
                        bi = beta + 3
                        partial = beta <= 0
                        jj = blk_j
                        sb_sl = jj % 2
                        p_sl = jj % 3
                        sbank = psA if jj % 2 == 0 else psB
                        em.w("tensor", c_v, stt_done.get(jj - 2, 0))
                        mm_s = em.op("tensor", lambda En, sbank=sbank, sl=sl, kt=kt, qg=qg: En.matmul(
                            sbank[:, :], ktb[:, sl * N + kt * 128: sl * N + (kt + 1) * 128],
                            qtb[:, sl * N + qg * 512: sl * N + (qg + 1) * 512],
                            start=True, stop=True), inc=c_t)
                        em.w("vector", c_t, mm_s)
                        em.w("vector", c_v, a_ready)
                        em.w("vector", c_s, exp_done.get(jj - 2, 0))
                        vst = em.op("vector", lambda En, sbank=sbank, sb_sl=sb_sl, bi=bi: En.scalar_tensor_tensor(
                            out=sbs[:, sb_sl * 512:(sb_sl + 1) * 512], in0=sbank[:, :],
                            scalar=0.0, in1=abl[:, bi * 512:(bi + 1) * 512],
                            op0=ALU.max, op1=ALU.mult), inc=c_v)
                        stt_done[jj] = vst
                        em.w("scalar", c_v, vst)
                        em.w("scalar", c_t, ctx_done.get(jj - 3, 0))
                        em.op("scalar", lambda En, sb_sl=sb_sl, p_sl=p_sl: En.activation(
                            pb[:, p_sl * 512:(p_sl + 1) * 512], sbs[:, sb_sl * 512:(sb_sl + 1) * 512],
                            AF.Exp))
                        sxp = em.drain("scalar", inc=c_s)
                        exp_done[jj] = sxp
                        rhs_buf = pb
                        if partial:
                            em.w("gpsimd", c_s, sxp)
                            em.w("gpsimd", c_t, pm_free.get(p_sl, 0))
                            em.op("gpsimd", lambda En, p_sl=p_sl, beta=beta: En.affine_select(
                                pm[:, p_sl * 512:(p_sl + 1) * 512], pb[:, p_sl * 512:(p_sl + 1) * 512],
                                [[1, 512]], ALU.is_ge, 0.0, base=128 * beta,
                                channel_multiplier=-1))
                            gsl = em.drain("gpsimd", inc=c_g)
                            rhs_buf = pm
                        if partial:
                            em.w("tensor", c_g, gsl)
                        else:
                            em.w("tensor", c_s, sxp)
                        if kt == 0:
                            em.w("tensor", c_v, ctx_bankfree_v[cxid])
                            em.w("tensor", c_s, ctx_bankfree_s[cxid])
                        mm_c = em.op("tensor", lambda En, cxbank=cxbank, sl=sl, kt=kt, rhs_buf=rhs_buf, p_sl=p_sl, nkt=nkt: En.matmul(
                            cxbank[0:65, :], vxb[:, sl * NT * 65 + kt * 65: sl * NT * 65 + (kt + 1) * 65],
                            rhs_buf[:, p_sl * 512:(p_sl + 1) * 512],
                            start=(kt == 0), stop=(kt == nkt - 1)), inc=c_t)
                        ctx_done[jj] = mm_c
                        if partial:
                            pm_free[p_sl] = mm_c
                        last_t_this = mm_c
                        blk_j += 1
                    em.w("vector", c_t, ctx_done[blk_j - 1])
                    em.w("vector", c_t, prev_tbc)
                    em.op("vector", lambda En, cxbank=cxbank: En.reciprocal(rd[:, :], cxbank[64:65, :]))
                    v_rd = em.drain("vector", inc=c_v)
                    em.w("tensor", c_v, v_rd)
                    em.w("tensor", c_v, prev_vev)
                    prev_tbc = em.op("tensor", lambda En: En.matmul(
                        psE2[0:64, :], ones_f[:, 0:64], rd[:, :], start=True, stop=True), inc=c_t)
                    em.w("scalar", c_t, ctx_done[blk_j - 1])
                    em.w("scalar", c_v, prev_vev)
                    em.op("scalar", lambda En, cxbank=cxbank: En.copy(sbctx[:, :], cxbank[0:64, :]))
                    s_cx = em.drain("scalar", inc=c_s)
                    em.w("vector", c_s, s_cx)
                    em.w("vector", c_t, prev_tbc)
                    em.w("vector", c_st, ctxn_free[cb])
                    em.op("vector", lambda En, cb=cb, qg=qg: En.tensor_tensor(
                        ctxn[:, cb * N + qg * 512: cb * N + (qg + 1) * 512],
                        sbctx[:, :], psE2[0:64, :], op=ALU.mult))
                    v_ev = em.drain("vector", inc=c_v)
                    prev_vev = v_ev
                    ctx_bankfree_v[cxid] = v_rd
                    ctx_bankfree_s[cxid] = s_cx
                slot_free_t[sl] = last_t_this
                if m + 2 < M:
                    issue_qkv_load(m + 2)
                abl_last_user = stt_done[blk_j - 1]
                em.w("gpsimd", c_v, v_ev)
                ctxn_free[cb] = em.op("gpsimd", lambda En, seq=seq, h=h, cb=cb: En.dma_start(
                    out=dap(cts, seq * D * N + h * 64 * N, [[N, 64], [1, N]]),
                    in_=ctxn[0:64, cb * N:(cb + 1) * N]), inc=c_st, k=16)
            em.flush()
    p2_done_st = c_st.n
    p2_done_v = c_v.n
    p2_done_s = c_s.n

    with nc.sbuf_tensor("acc", [128, NT * 1024], F32) as acc, \
         nc.sbuf_tensor("y1t", [128, DT * N], BF16) as y1t, \
         nc.sbuf_tensor("arena", [128, 16384], BF16) as arena, \
         nc.sbuf_tensor("w1q", [128, 8192], BF16) as w1q, \
         nc.sbuf_tensor("w2q", [128, 8192], BF16) as w2q, \
         nc.sbuf_tensor("xres", [128, 2 * 1024], F32) as xres, \
         nc.sbuf_tensor("y1stg", [128, 1024], BF16) as y1stg, \
         nc.sbuf_tensor("logits", [128, NT * E], F32) as logits, \
         nc.sbuf_tensor("masks", [128, NT * E], F32) as masks, \
         nc.sbuf_tensor("mkt", [8, 128], F32) as mkt, \
         nc.sbuf_tensor("gwb", [128, D], F32) as gwb, \
         nc.sbuf_tensor("gsc", [128, D], F32) as gsc, \
         nc.sbuf_tensor("stats", [128, 8 * NT], F32) as stats, \
         nc.sbuf_tensor("cols", [128, 8], F32) as cols:
        SUM0, SUM1, SSQ0, SSQ1, MEAN, RSTD, TMP, NF = range(8)

        def stat(k, qt):
            return stats[:, k * NT + qt: k * NT + qt + 1]

        def statblk(k):
            return stats[:, k * NT:(k + 1) * NT]

        w1_ready = 0
        w2_ready = 0
        m1_done_prev = 0
        m2_done_prev = 0

        def load_w1(e, fq):
            nonlocal w1_ready
            em.w("sync", c_t, m1_done_prev)
            w1_ready = em.op("sync", lambda En, e=e, fq=fq: En.dma_start(
                out=w1q[:, :],
                in_=dap(w1b, e * D * DFF + fq * 1024, [[DFF, 128], [128 * DFF, DT], [1, 1024]])), inc=c_ld, k=16)

        def load_w2(e, fq):
            nonlocal w2_ready
            em.w("sync", c_t, m2_done_prev)
            w2_ready = em.op("sync", lambda En, e=e, fq=fq: En.dma_start(
                out=w2q[:, :],
                in_=dap(w2b, e * DFF * D + fq * 1024 * D, [[D, 128], [128 * D, 8], [1, 1024]])), inc=c_ld, k=16)

        load_w1(0, 0)
        load_w2(0, 0)

        arena_last_use = 0
        y1t_last_use = 0
        gwb_used = 0
        prev_out_last = 0
        for seq in range(S):
            em.w("sync", c_st, p2_done_st)
            if seq > 0:
                em.w("sync", c_t, arena_last_use)
            lctx = em.op("sync", lambda En, seq=seq: En.dma_start(
                out=arena[:, :],
                in_=dap(cts, seq * D * N, [[N, 128], [128 * N, DT], [1, N]])), inc=c_ld, k=16)
            xres_free = [0, 0]
            wo_evacs = {}
            mm = 0
            for qt in range(NT):
                xsl = qt % 2
                em.w("sync", c_v, xres_free[xsl])
                lxr = em.op("sync", lambda En, seq=seq, qt=qt, xsl=xsl: En.dma_start(
                    out=xres[:, xsl * 1024:(xsl + 1) * 1024],
                    in_=dap(xin, seq * N * D + qt * 128 * D, [[D, 128], [1, D]])), inc=c_ld, k=16)
                for ch in range(2):
                    bank = psA if ch == 0 else psB
                    em.w("tensor", c_ld, lctx)
                    em.w("tensor", c_ld, prepass_ld)
                    em.w("tensor", c_v, prepass_v)
                    em.w("tensor", c_v, p2_done_v)
                    em.w("tensor", c_s, p2_done_s)
                    if seq > 0 and qt == 0:
                        em.w("tensor", c_st, prev_out_last)
                    em.w("tensor", c_v, wo_evacs.get((qt - 1, ch), 0))
                    for dk in range(DT):
                        mm = em.op("tensor", lambda En, bank=bank, dk=dk, qt=qt, ch=ch: En.matmul(
                            bank[:, :], arena[:, dk * N + qt * 128: dk * N + (qt + 1) * 128],
                            wo_b[:, dk * D + ch * 512: dk * D + (ch + 1) * 512],
                            start=(dk == 0), stop=(dk == DT - 1)), inc=c_t)
                    em.w("vector", c_t, mm)
                    em.w("vector", c_ld, lxr)
                    if seq > 0 and qt == 0 and ch == 0:
                        em.w("vector", c_st, prev_out_last)
                    em.op("vector", lambda En, bank=bank, qt=qt, ch=ch: En.scalar_tensor_tensor(
                        out=acc[:, qt * 1024 + ch * 512: qt * 1024 + (ch + 1) * 512],
                        in0=bank[:, :], scalar=1.0, in1=bobc[:, ch * 512:(ch + 1) * 512],
                        op0=ALU.mult, op1=ALU.add), inc=c_v)
                    em.op("vector", lambda En, qt=qt, ch=ch, xsl=xsl: En.scalar_tensor_tensor(
                        out=acc[:, qt * 1024 + ch * 512: qt * 1024 + (ch + 1) * 512],
                        in0=acc[:, qt * 1024 + ch * 512: qt * 1024 + (ch + 1) * 512],
                        scalar=1.0, in1=xres[:, xsl * 1024 + ch * 512: xsl * 1024 + (ch + 1) * 512],
                        op0=ALU.mult, op1=ALU.add,
                        accum_out=stat(SUM0 + ch, qt)), inc=c_v)
                    ev = em.op("vector", lambda En, qt=qt, ch=ch: En.scalar_tensor_tensor(
                        out=gsc[:, ch * 512:(ch + 1) * 512],
                        in0=acc[:, qt * 1024 + ch * 512: qt * 1024 + (ch + 1) * 512],
                        scalar=1.0, in1=acc[:, qt * 1024 + ch * 512: qt * 1024 + (ch + 1) * 512],
                        op0=ALU.mult, op1=ALU.mult,
                        accum_out=stat(SSQ0 + ch, qt)), inc=c_v)
                    wo_evacs[(qt, ch)] = ev
                xres_free[xsl] = wo_evacs[(qt, 1)]
            em.w("vector", c_v, wo_evacs[(NT - 1, 1)])
            em.drain("vector")
            em.op("vector", lambda En: En.tensor_tensor(statblk(MEAN), statblk(SUM0), statblk(SUM1), op=ALU.add))
            em.drain("vector")
            em.op("vector", lambda En: En.tensor_scalar_mul(statblk(MEAN), statblk(MEAN), 1.0 / D))
            em.drain("vector")
            em.op("vector", lambda En: En.tensor_tensor(statblk(TMP), statblk(SSQ0), statblk(SSQ1), op=ALU.add))
            em.drain("vector")
            em.op("vector", lambda En: En.tensor_scalar_mul(statblk(TMP), statblk(TMP), 1.0 / D))
            em.drain("vector")
            em.op("vector", lambda En: En.tensor_tensor(statblk(RSTD), statblk(MEAN), statblk(MEAN), op=ALU.mult))
            em.drain("vector")
            em.op("vector", lambda En: En.tensor_tensor(statblk(TMP), statblk(TMP), statblk(RSTD), op=ALU.subtract))
            v_var = em.drain("vector", inc=c_v)
            em.w("scalar", c_v, v_var)
            em.op("scalar", lambda En: En.activation(statblk(RSTD), statblk(TMP), AF.Sqrt, bias=epsc[:, 0:1]))
            s_sd = em.drain("scalar", inc=c_s)
            em.w("vector", c_s, s_sd)
            em.op("vector", lambda En: En.reciprocal(statblk(RSTD), statblk(RSTD)))
            v_rstd = em.drain("vector", inc=c_v)
            y1_tr = {}
            cst = 0
            for qt in range(NT):
                em.w("vector", c_v, v_rstd)
                nrm = em.op("vector", lambda En, qt=qt: En.tensor_scalar(
                    out=acc[:, qt * 1024:(qt + 1) * 1024], in0=acc[:, qt * 1024:(qt + 1) * 1024],
                    scalar1=stat(MEAN, qt), scalar2=stat(RSTD, qt),
                    op0=ALU.subtract, op1=ALU.mult), inc=c_v)
                em.w("scalar", c_v, nrm)
                em.w("scalar", c_v, y1_tr.get(qt - 1, 0))
                em.op("scalar", lambda En, qt=qt: En.copy(y1stg[:, :], acc[:, qt * 1024:(qt + 1) * 1024]))
                cst = em.drain("scalar", inc=c_s)
                em.w("tensor", c_s, cst)
                if qt >= 1:
                    em.w("tensor", c_v, y1_tr[qt - 1])
                if seq > 0 and qt == 0:
                    em.w("tensor", c_t, y1t_last_use)
                ttr = 0
                for dt in range(DT):
                    ttr = em.op("tensor", lambda En, dt=dt: En.transpose(
                        psT[:, dt * 128:(dt + 1) * 128], y1stg[:, dt * 128:(dt + 1) * 128],
                        eye_b[:, :]), inc=c_t)
                em.w("vector", c_t, ttr)
                if seq > 0 and qt == 0:
                    em.w("vector", c_t, y1t_last_use)
                y1_tr[qt] = em.op("vector", lambda En, qt=qt: En.tensor_copy(
                    dap(y1t, qt * 128, [[DT * N, 128], [N, DT], [1, 128]]), psT[:, :]), inc=c_v)
            y1t_ready = y1_tr[NT - 1]
            lg = 0
            gw_row_reads = 0
            for e in range(E):
                em.w("sync", c_v, wo_evacs[(NT - 1, 1)])
                em.w("sync", c_t, gw_row_reads)
                lgw = em.op("sync", lambda En, e=e: En.dma_start(
                    out=xres[0:1, 0:1024], in_=dap(gw, e * D, [[D, 1], [1, D]])), inc=c_ld, k=16)
                em.w("tensor", c_ld, lgw)
                em.w("tensor", c_v, prepass_v)
                if gwb_used:
                    em.w("tensor", c_v, gwb_used)
                em.op("tensor", lambda En, e=e: En.matmul(
                    psE2[:, :], ones_f[:, :], xres[0:1, 0:512], start=True, stop=True), inc=c_t)
                t_g2 = em.op("tensor", lambda En, e=e: En.matmul(
                    psD[:, :], ones_f[:, :], xres[0:1, 512:1024],
                    start=True, stop=True), inc=c_t)
                gw_row_reads = t_g2
                em.w("vector", c_t, t_g2)
                em.op("vector", lambda En: En.tensor_copy(gwb[:, 0:512], psE2[:, 0:512]), inc=c_v)
                v_gw = em.op("vector", lambda En: En.tensor_copy(gwb[:, 512:1024], psD[:, 0:512]), inc=c_v)
                for qt in range(NT):
                    lg = em.op("vector", lambda En, qt=qt, e=e: En.scalar_tensor_tensor(
                        out=gsc[:, :], in0=acc[:, qt * 1024:(qt + 1) * 1024], scalar=1.0,
                        in1=gwb[:, :], op0=ALU.mult, op1=ALU.mult,
                        accum_out=logits[:, qt * E + e: qt * E + e + 1]), inc=c_v)
                gwb_used = lg
            logits_done = lg
            em.drain("vector")
            mask_v = {}
            for qt in range(NT):
                em.op("vector", lambda En, qt=qt: En.tensor_tensor(
                    logits[:, qt * E:(qt + 1) * E], logits[:, qt * E:(qt + 1) * E], gbb[:, :], op=ALU.add))
                em.drain("vector")
                em.op("vector", lambda En, qt=qt: En.reduce_max(cols[:, 0:1], logits[:, qt * E:(qt + 1) * E], axis=AX.X))
                em.drain("vector")
                em.op("vector", lambda En: En.memset(cols[:, 1:2], 1.0))
                em.drain("vector")
                for e in range(E):
                    em.op("vector", lambda En, qt=qt, e=e: En.tensor_tensor(
                        cols[:, 2:3], logits[:, qt * E + e: qt * E + e + 1], cols[:, 0:1], op=ALU.is_ge))
                    em.drain("vector")
                    em.op("vector", lambda En, qt=qt, e=e: En.tensor_tensor(
                        masks[:, qt * E + e: qt * E + e + 1], cols[:, 2:3], cols[:, 1:2], op=ALU.mult))
                    em.drain("vector")
                    em.op("vector", lambda En, qt=qt, e=e: En.tensor_tensor(
                        cols[:, 1:2], cols[:, 1:2], masks[:, qt * E + e: qt * E + e + 1], op=ALU.subtract))
                    em.drain("vector")
                mask_v[qt] = em.drain("vector", inc=c_v)
            b2_evac = 0
            b2add = 0
            mkt_reads = 0
            for qt in range(NT):
                em.w("tensor", c_v, mask_v[qt])
                em.w("tensor", c_v, b2_evac)
                tmk = em.op("tensor", lambda En, qt=qt: En.transpose(
                    psC[0:8, 0:128], masks[:, qt * E:(qt + 1) * E], eye_f[:, :]), inc=c_t)
                em.w("vector", c_t, tmk)
                em.w("vector", c_t, mkt_reads)
                b2_evac = em.op("vector", lambda En, qt=qt: En.tensor_copy(
                    mkt[:, :], psC[0:8, 0:128]), inc=c_v)
                for ch in range(2):
                    em.w("tensor", c_v, b2_evac)
                    if b2add:
                        em.w("tensor", c_v, b2add)
                    tb2 = em.op("tensor", lambda En, qt=qt, ch=ch: En.matmul(
                        psD[:, :], mkt[:, :], b2s[:, ch * 512:(ch + 1) * 512],
                        start=True, stop=True), inc=c_t)
                    mkt_reads = tb2
                    em.w("vector", c_t, tb2)
                    b2add = em.op("vector", lambda En, qt=qt, ch=ch: En.scalar_tensor_tensor(
                        out=acc[:, qt * 1024 + ch * 512: qt * 1024 + (ch + 1) * 512],
                        in0=psD[:, :], scalar=1.0,
                        in1=acc[:, qt * 1024 + ch * 512: qt * 1024 + (ch + 1) * 512],
                        op0=ALU.mult, op1=ALU.add), inc=c_v)
            acc_base = b2add
            em.flush()

            h_evac = {}
            acc_upd = {}
            for e in range(E):
                for fq in range(4):
                    w1_now, w2_now = w1_ready, w2_ready
                    m1_last = 0
                    for ft in range(8):
                        for tb in range(4):
                            idx = ft * 4 + tb
                            bank = psA if idx % 2 == 0 else psB
                            em.w("tensor", c_ld, w1_now)
                            em.w("tensor", c_v, y1t_ready)
                            if e == 0 and fq == 0:
                                em.w("tensor", c_v, acc_base)
                            em.w("tensor", c_s, h_evac.get(idx - 2, 0))
                            for dk in range(DT):
                                mmx = em.op("tensor", lambda En, bank=bank, ft=ft, tb=tb, dk=dk: En.matmul(
                                    bank[:, :], w1q[:, dk * 1024 + ft * 128: dk * 1024 + (ft + 1) * 128],
                                    y1t[:, dk * N + tb * 512: dk * N + (tb + 1) * 512],
                                    start=(dk == 0), stop=(dk == DT - 1)), inc=c_t)
                            m1_last = mmx
                            em.w("scalar", c_t, mmx)
                            em.w("scalar", c_t, m2_done_prev if idx == 0 else 0)
                            if idx == 31:
                                em.op("scalar", lambda En, bank=bank, ft=ft, tb=tb, e=e, fq=fq: En.activation(
                                    arena[:, ft * N + tb * 512: ft * N + (tb + 1) * 512], bank[:, :],
                                    AF.Relu, bias=b1s[:, (fq * 8 + ft) * E + e: (fq * 8 + ft) * E + e + 1]))
                                h_evac[idx] = em.drain("scalar", inc=c_s)
                            else:
                                h_evac[idx] = em.op("scalar", lambda En, bank=bank, ft=ft, tb=tb, e=e, fq=fq: En.activation(
                                    arena[:, ft * N + tb * 512: ft * N + (tb + 1) * 512], bank[:, :],
                                    AF.Relu, bias=b1s[:, (fq * 8 + ft) * E + e: (fq * 8 + ft) * E + e + 1]), inc=c_s)
                    m1_done_prev = m1_last
                    h_ready = h_evac[31]
                    nxt = e * 4 + fq + 1
                    if nxt < 32:
                        load_w1(nxt // 4, nxt % 4)
                    elif seq < S - 1:
                        load_w1(0, 0)
                    m2_last = 0
                    for tt in range(NT):
                        for ch in range(2):
                            idx = tt * 2 + ch
                            bank = psC if idx % 2 == 0 else psD
                            em.w("tensor", c_ld, w2_now)
                            em.w("tensor", c_s, h_ready)
                            em.w("tensor", c_v, acc_upd.get(idx - 2, 0))
                            for ft in range(8):
                                mmx = em.op("tensor", lambda En, bank=bank, tt=tt, ch=ch, ft=ft: En.matmul(
                                    bank[:, :], arena[:, ft * N + tt * 128: ft * N + (tt + 1) * 128],
                                    w2q[:, ft * 1024 + ch * 512: ft * 1024 + (ch + 1) * 512],
                                    start=(ft == 0), stop=(ft == 7)), inc=c_t)
                            m2_last = mmx
                            em.w("vector", c_t, mmx)
                            em.w("vector", c_v, acc_base)
                            acc_upd[idx] = em.op("vector", lambda En, bank=bank, tt=tt, ch=ch, e=e: En.scalar_tensor_tensor(
                                out=acc[:, tt * 1024 + ch * 512: tt * 1024 + (ch + 1) * 512],
                                in0=bank[:, :], scalar=masks[:, tt * E + e: tt * E + e + 1],
                                in1=acc[:, tt * 1024 + ch * 512: tt * 1024 + (ch + 1) * 512],
                                op0=ALU.mult, op1=ALU.add), inc=c_v)
                    m2_done_prev = m2_last
                    if nxt < 32:
                        load_w2(nxt // 4, nxt % 4)
                    elif seq < S - 1:
                        load_w2(0, 0)
                em.flush()
            moe_done_v = acc_upd[NT * 2 - 1]
            y1t_last_use = m1_done_prev
            arena_last_use = m2_done_prev

            for qt in range(NT):
                em.w("vector", c_v, moe_done_v)
                em.op("vector", lambda En, qt=qt: En.tensor_scalar(
                    out=gsc[:, :], in0=acc[:, qt * 1024:(qt + 1) * 1024], scalar1=1.0, scalar2=0.0,
                    op0=ALU.mult, op1=ALU.add, accum_out=stat(SUM0, qt)), inc=c_v)
                em.op("vector", lambda En, qt=qt: En.scalar_tensor_tensor(
                    out=gsc[:, :], in0=acc[:, qt * 1024:(qt + 1) * 1024], scalar=1.0,
                    in1=acc[:, qt * 1024:(qt + 1) * 1024], op0=ALU.mult, op1=ALU.mult,
                    accum_out=stat(SSQ0, qt)), inc=c_v)
            em.drain("vector")
            em.op("vector", lambda En: En.tensor_scalar_mul(statblk(MEAN), statblk(SUM0), 1.0 / D))
            em.drain("vector")
            em.op("vector", lambda En: En.tensor_scalar_mul(statblk(TMP), statblk(SSQ0), 1.0 / D))
            em.drain("vector")
            em.op("vector", lambda En: En.tensor_tensor(statblk(RSTD), statblk(MEAN), statblk(MEAN), op=ALU.mult))
            em.drain("vector")
            em.op("vector", lambda En: En.tensor_tensor(statblk(TMP), statblk(TMP), statblk(RSTD), op=ALU.subtract))
            v_var2 = em.drain("vector", inc=c_v)
            em.w("scalar", c_v, v_var2)
            em.op("scalar", lambda En: En.activation(statblk(RSTD), statblk(TMP), AF.Sqrt, bias=epsc[:, 0:1]))
            s_sd2 = em.drain("scalar", inc=c_s)
            em.w("vector", c_s, s_sd2)
            em.op("vector", lambda En: En.reciprocal(statblk(RSTD), statblk(RSTD)))
            v_rstd2 = em.drain("vector", inc=c_v)
            for qt in range(NT):
                em.w("vector", c_v, v_rstd2)
                nout = em.op("vector", lambda En, qt=qt: En.tensor_scalar(
                    out=acc[:, qt * 1024:(qt + 1) * 1024], in0=acc[:, qt * 1024:(qt + 1) * 1024],
                    scalar1=stat(MEAN, qt), scalar2=stat(RSTD, qt),
                    op0=ALU.subtract, op1=ALU.mult), inc=c_v)
                em.w("gpsimd", c_v, nout)
                prev_out_last = em.op("gpsimd", lambda En, seq=seq, qt=qt: En.dma_start(
                    out=dap(yout, seq * N * D + qt * 128 * D, [[D, 128], [1, D]]),
                    in_=acc[:, qt * 1024:(qt + 1) * 1024]), inc=c_st, k=16)
            em.flush()

        em.w("gpsimd", c_st, c_st.n)
        em.flush()
    es.close()
    return nc


_CACHE = {}


def kernel(L0_attn_x, L0_attn_Wq, L0_attn_Wk, L0_attn_Wv, L0_attn_Wo, L0_attn_bo,
           L0_attn_w, L0_attn_v, L0_ffn_gate_w, L0_ffn_gate_b, L0_ffn_expert_bias,
           L0_ffn_w1, L0_ffn_b1, L0_ffn_w2, L0_ffn_b2):
    if "nc" not in _CACHE:
        _CACHE["nc"] = build()
    nc = _CACHE["nc"]

    def f32(a):
        return np.ascontiguousarray(np.asarray(a, dtype=np.float32))

    x = f32(L0_attn_x)
    b1 = f32(L0_ffn_b1)
    b1rr = np.ascontiguousarray(b1.T.reshape(FT, 128, E).transpose(1, 0, 2).reshape(128, FT * E))
    common = {
        "wq": f32(L0_attn_Wq), "wk": f32(L0_attn_Wk), "wv": f32(L0_attn_Wv),
        "wo": f32(L0_attn_Wo), "bo": f32(L0_attn_bo).reshape(1, D),
        "wdec": f32(L0_attn_w).reshape(1, H), "vdec": f32(L0_attn_v).reshape(1, H),
        "gw": f32(L0_ffn_gate_w), "gb": f32(L0_ffn_gate_b).reshape(1, E),
        "ebias": f32(L0_ffn_expert_bias).reshape(1, E),
        "w1": f32(L0_ffn_w1), "b1r": b1rr, "w2": f32(L0_ffn_w2), "b2": f32(L0_ffn_b2),
        "eye": np.eye(128, dtype=np.float32),
        "onesr": np.ones((1, 128), dtype=np.float32),
    }
    in_maps = []
    for c in range(8):
        m = dict(common)
        m["x"] = np.ascontiguousarray(x[c * S:(c + 1) * S])
        in_maps.append(m)
    res = run_bass_kernel_spmd(nc, in_maps, core_ids=list(range(8)))
    out = np.concatenate([np.asarray(r["y"], dtype=np.float32) for r in res.results], axis=0)
    return out.reshape(32, N, D)


# version 05
# speedup vs baseline: 1.0198x; 1.0198x over previous
import numpy as np
from contextlib import ExitStack

import concourse.bass as bass
import concourse.mybir as mybir
from concourse.bass_utils import run_bass_kernel_spmd

F32 = mybir.dt.float32
BF16 = mybir.dt.bfloat16
AF = mybir.ActivationFunctionType
ALU = mybir.AluOpType
AX = mybir.AxisListType

S = 4
N = 2048
D = 1024
H = 16
E = 8
DFF = 4096
NT = N // 128
DT = D // 128
FT = DFF // 128

ENGINES = ("sync", "tensor", "vector", "scalar", "gpsimd")


class Ctr:
    def __init__(self, nc, es, name):
        self.h = es.enter_context(nc.semaphore(name))
        self.name = name
        self.n = 0

    def bump(self, k=1):
        self.n += k
        return self.n


class EM:
    def __init__(self, nc):
        self.nc = nc
        self.q = {e: [] for e in ENGINES}
        self.lastwait = {}

    def w(self, eng, ctr, val):
        if val is None or val <= 0:
            return
        assert val <= ctr.n, f"wait on future count {ctr.name} {val} > {ctr.n}"
        key = (eng, ctr.name)
        if self.lastwait.get(key, 0) >= val:
            return
        self.lastwait[key] = val
        self.q[eng].append(lambda Eng, h=ctr.h, v=val: Eng.wait_ge(h, v))

    def op(self, eng, fn, inc=None, k=1):
        if inc is not None:
            val = inc.bump(k)

            def emit(Eng, fn=fn, h=inc.h, k=k):
                fn(Eng).then_inc(h, k)
            self.q[eng].append(emit)
            return val
        self.q[eng].append(lambda Eng, fn=fn: fn(Eng))
        return None

    def drain(self, eng, inc=None):
        return self.op(eng, lambda Eng: Eng.drain(), inc=inc)

    def flush(self):
        qs = self.q
        self.q = {e: [] for e in ENGINES}
        if not any(qs.values()):
            return
        with self.nc.Block() as blk:
            for e in ENGINES:
                fns = qs[e]
                if fns:
                    getattr(blk, e)(lambda Eng, fns=fns: [f(Eng) for f in fns])


def build():
    nc = bass.Bass("TRN2", target_bir_lowering=False)

    xin = nc.dram_tensor("x", [S, N, D], F32, kind="ExternalInput")
    wq = nc.dram_tensor("wq", [D, D], F32, kind="ExternalInput")
    wk = nc.dram_tensor("wk", [D, D], F32, kind="ExternalInput")
    wv = nc.dram_tensor("wv", [D, D], F32, kind="ExternalInput")
    wo = nc.dram_tensor("wo", [D, D], F32, kind="ExternalInput")
    bo = nc.dram_tensor("bo", [1, D], F32, kind="ExternalInput")
    wdec = nc.dram_tensor("wdec", [1, H], F32, kind="ExternalInput")
    vdec = nc.dram_tensor("vdec", [1, H], F32, kind="ExternalInput")
    gw = nc.dram_tensor("gw", [E, D], F32, kind="ExternalInput")
    gb = nc.dram_tensor("gb", [1, E], F32, kind="ExternalInput")
    ebias = nc.dram_tensor("ebias", [1, E], F32, kind="ExternalInput")
    w1 = nc.dram_tensor("w1", [E, D, DFF], F32, kind="ExternalInput")
    b1r = nc.dram_tensor("b1r", [128, FT * E], F32, kind="ExternalInput")
    w2 = nc.dram_tensor("w2", [E, DFF, D], F32, kind="ExternalInput")
    b2 = nc.dram_tensor("b2", [E, D], F32, kind="ExternalInput")
    eye = nc.dram_tensor("eye", [128, 128], F32, kind="ExternalInput")
    onesr = nc.dram_tensor("onesr", [1, 128], F32, kind="ExternalInput")
    yout = nc.dram_tensor("y", [S, N, D], F32, kind="ExternalOutput")

    w1b = nc.dram_tensor("w1b", [E, D, DFF], BF16)
    w2b = nc.dram_tensor("w2b", [E, DFF, D], BF16)
    qts = nc.dram_tensor("qts", [S, D, N], BF16)
    kts = nc.dram_tensor("kts", [S, D, N], BF16)
    vs = nc.dram_tensor("vs", [S, N, D], BF16)
    cts = nc.dram_tensor("cts", [S, D, N], BF16)

    def dap(t, off, dims):
        return bass.AP(t, off, dims)

    es = ExitStack()
    em = EM(nc)
    c_ld = Ctr(nc, es, "c_ld")
    c_st = Ctr(nc, es, "c_st")
    c_t = Ctr(nc, es, "c_t")
    c_v = Ctr(nc, es, "c_v")
    c_s = Ctr(nc, es, "c_s")
    c_g = Ctr(nc, es, "c_g")

    eye_f = es.enter_context(nc.sbuf_tensor("eye_f", [128, 128], F32))
    eye_b = es.enter_context(nc.sbuf_tensor("eye_b", [128, 128], BF16))
    ones_f = es.enter_context(nc.sbuf_tensor("ones_f", [1, 128], F32))
    rowbuf = es.enter_context(nc.sbuf_tensor("rowbuf", [1, 2 * H], F32))
    grow = es.enter_context(nc.sbuf_tensor("grow", [1, 2 * E], F32))
    bobc = es.enter_context(nc.sbuf_tensor("bobc", [128, D], F32))
    gbb = es.enter_context(nc.sbuf_tensor("gbb", [128, E], F32))
    wcol = es.enter_context(nc.sbuf_tensor("wcol", [128, H], F32))
    nvcol = es.enter_context(nc.sbuf_tensor("nvcol", [128, H], F32))
    c8col = es.enter_context(nc.sbuf_tensor("c8col", [128, H], F32))
    b1s = es.enter_context(nc.sbuf_tensor("b1s", [128, FT * E], F32))
    b2s = es.enter_context(nc.sbuf_tensor("b2s", [E, D], F32))
    epsc = es.enter_context(nc.sbuf_tensor("epsc", [128, 1], F32))

    psA = es.enter_context(nc.psum_tensor("psA", [128, 512], F32))
    psB = es.enter_context(nc.psum_tensor("psB", [128, 512], F32))
    psC = es.enter_context(nc.psum_tensor("psC", [128, 512], F32))
    psD = es.enter_context(nc.psum_tensor("psD", [128, 512], F32))
    psE2 = es.enter_context(nc.psum_tensor("psE2", [128, 512], F32))
    psT = es.enter_context(nc.psum_tensor("psT", [128, 1024], BF16))

    em.op("sync", lambda En: En.dma_start(out=eye_f[:, :], in_=eye.ap()), inc=c_ld, k=16)
    em.op("sync", lambda En: En.dma_start(out=ones_f[:, :], in_=onesr.ap()), inc=c_ld, k=16)
    em.op("sync", lambda En: En.dma_start(out=rowbuf[:, 0:H], in_=wdec.ap()), inc=c_ld, k=16)
    em.op("sync", lambda En: En.dma_start(out=rowbuf[:, H:2 * H], in_=vdec.ap()), inc=c_ld, k=16)
    em.op("sync", lambda En: En.dma_start(out=grow[:, 0:E], in_=gb.ap()), inc=c_ld, k=16)
    em.op("sync", lambda En: En.dma_start(out=grow[:, E:2 * E], in_=ebias.ap()), inc=c_ld, k=16)
    em.op("sync", lambda En: En.dma_start(out=b1s[:, :], in_=b1r.ap()), inc=c_ld, k=16)
    ld0 = em.op("sync", lambda En: En.dma_start(out=b2s[:, :], in_=b2.ap()), inc=c_ld, k=16)

    em.w("vector", c_ld, ld0)
    em.op("vector", lambda En: En.memset(epsc[:, :], 1e-5), inc=c_v)
    em.op("vector", lambda En: En.tensor_copy(eye_b[:, :], eye_f[:, :]), inc=c_v)
    em.op("vector", lambda En: En.tensor_tensor(grow[:, 0:E], grow[:, 0:E], grow[:, E:2 * E], op=ALU.add))
    v_grow = em.drain("vector", inc=c_v)

    em.w("tensor", c_ld, ld0)
    em.w("tensor", c_v, v_grow)
    t_b1 = em.op("tensor", lambda En: En.matmul(psA[:, 0:2 * H], ones_f[:, :], rowbuf[:, :], start=True, stop=True), inc=c_t)
    t_b2 = em.op("tensor", lambda En: En.matmul(psB[:, 0:E], ones_f[:, :], grow[:, 0:E], start=True, stop=True), inc=c_t)
    em.w("vector", c_t, t_b2)
    em.op("vector", lambda En: En.tensor_copy(wcol[:, :], psA[:, 0:H]), inc=c_v)
    em.op("vector", lambda En: En.tensor_scalar_mul(nvcol[:, :], psA[:, H:2 * H], -1.0), inc=c_v)
    em.op("vector", lambda En: En.tensor_copy(gbb[:, :], psB[:, 0:E]), inc=c_v)
    em.w("scalar", c_t, t_b1)
    s_ev = em.op("scalar", lambda En: En.activation(c8col[:, :], psA[:, H:2 * H], AF.Exp), inc=c_s)
    em.w("vector", c_s, s_ev)
    em.op("vector", lambda En: En.tensor_scalar(out=c8col[:, :], in0=c8col[:, :], scalar1=1.0, scalar2=0.125, op0=ALU.add, op1=ALU.mult))
    v_c8 = em.drain("vector", inc=c_v)

    wo_b = es.enter_context(nc.sbuf_tensor("wo_b", [128, DT * D], BF16))
    es_p1 = ExitStack()
    wq_b = es_p1.enter_context(nc.sbuf_tensor("wq_b", [128, DT * D], BF16))
    wk_b = es_p1.enter_context(nc.sbuf_tensor("wk_b", [128, DT * D], BF16))
    wv_b = es_p1.enter_context(nc.sbuf_tensor("wv_b", [128, DT * D], BF16))

    with nc.sbuf_tensor("pp_f", [128, 2 * 8192], F32) as pp_f, \
         nc.sbuf_tensor("pp_b", [128, 2 * 8192], BF16) as pp_b, \
         nc.sbuf_tensor("borow", [1, D], F32) as borow:
        ldbo = em.op("sync", lambda En: En.dma_start(out=borow[:, :], in_=bo.ap()), inc=c_ld, k=16)
        em.w("tensor", c_ld, ldbo)
        em.w("tensor", c_v, v_c8)
        em.op("tensor", lambda En: En.matmul(psC[:, 0:512], ones_f[:, :], borow[:, 0:512], start=True, stop=True), inc=c_t)
        t_b3 = em.op("tensor", lambda En: En.matmul(psD[:, 0:512], ones_f[:, :], borow[:, 512:1024], start=True, stop=True), inc=c_t)
        em.w("vector", c_t, t_b3)
        em.op("vector", lambda En: En.tensor_copy(bobc[:, 0:512], psC[:, 0:512]), inc=c_v)
        em.op("vector", lambda En: En.tensor_copy(bobc[:, 512:1024], psD[:, 0:512]), inc=c_v)
        prev_cast = 0
        for (src, dstb) in ((wq, wq_b), (wk, wk_b), (wv, wv_b), (wo, wo_b)):
            em.w("sync", c_v, prev_cast)
            lv = em.op("sync", lambda En, src=src: En.dma_start(
                out=pp_f[:, 0:8192],
                in_=dap(src, 0, [[D, 128], [128 * D, DT], [1, D]])), inc=c_ld, k=16)
            em.w("vector", c_ld, lv)
            prev_cast = em.op("vector", lambda En, dstb=dstb: En.tensor_copy(dstb[:, :], pp_f[:, 0:8192]), inc=c_v)

        cast_jobs = []
        for i in range(E * D // 128):
            cast_jobs.append((w1, w1b, i * 128 * DFF, DFF))
        for i in range(E * DFF // 128):
            cast_jobs.append((w2, w2b, i * 128 * D, D))
        half = 8192
        slot_cast = [prev_cast, prev_cast]
        slot_store = [0, 0]
        for j, (src, dst, off, width) in enumerate(cast_jobs):
            sl = j % 2
            em.w("sync", c_v, slot_cast[sl])
            lv = em.op("sync", lambda En, src=src, off=off, width=width, sl=sl: En.dma_start(
                out=pp_f[:, sl * half: sl * half + width],
                in_=dap(src, off, [[width, 128], [1, width]])), inc=c_ld, k=16)
            em.w("vector", c_ld, lv)
            em.w("vector", c_st, slot_store[sl])
            cv = em.op("vector", lambda En, sl=sl, width=width: En.tensor_copy(
                pp_b[:, sl * half: sl * half + width], pp_f[:, sl * half: sl * half + width]), inc=c_v)
            slot_cast[sl] = cv
            em.w("gpsimd", c_v, cv)
            slot_store[sl] = em.op("gpsimd", lambda En, dst=dst, off=off, width=width, sl=sl: En.dma_start(
                out=dap(dst, off, [[width, 128], [1, width]]),
                in_=pp_b[:, sl * half: sl * half + width]), inc=c_st, k=16)
        em.flush()

    prepass_v = c_v.n
    prepass_s = c_s.n
    prepass_ld = c_ld.n
    prepass_st = c_st.n

    p1_seq_st = []
    proj_last_t = 0
    p1_last_evac = 0
    stg_free = [0, 0, 0, 0]
    evacs = []
    with nc.sbuf_tensor("xt_b", [128, DT * N], BF16) as xt_b, \
         nc.sbuf_tensor("stg", [128, 4 * 512], BF16) as stg:
        for seq in range(S):
            with nc.sbuf_tensor(f"xf{seq}", [128, NT * 1024], F32) as xf:
                if seq > 0:
                    em.w("sync", c_t, proj_last_t)
                lx = em.op("sync", lambda En, seq=seq: En.dma_start(
                    out=xf[:, :],
                    in_=dap(xin, seq * N * D, [[D, 128], [128 * D, NT], [1, D]])), inc=c_ld, k=16)
                tr_evac = []
                for idx in range(NT * 2):
                    tt, hb = divmod(idx, 2)
                    bank = psA if idx % 2 == 0 else psB
                    em.w("tensor", c_ld, lx)
                    em.w("tensor", c_v, prepass_v)
                    if idx >= 2:
                        em.w("tensor", c_v, tr_evac[idx - 2])
                    elif seq > 0:
                        em.w("tensor", c_v, p1_last_evac)
                    for q in range(4):
                        dt = hb * 4 + q
                        t_tr = em.op("tensor", lambda En, bank=bank, q=q, tt=tt, dt=dt: En.transpose(
                            bank[:, q * 128:(q + 1) * 128],
                            xf[:, tt * 1024 + dt * 128: tt * 1024 + (dt + 1) * 128],
                            eye_f[:, :]), inc=c_t)
                    em.w("vector", c_t, t_tr)
                    ev = em.op("vector", lambda En, bank=bank, tt=tt, hb=hb: En.tensor_copy(
                        dap(xt_b, hb * 4 * N + tt * 128, [[DT * N, 128], [N, 4], [1, 128]]),
                        bank[:, 0:512]), inc=c_v)
                    tr_evac.append(ev)
                x_tr_done = tr_evac[-1]

            si = 0
            for (wgt, dst, transposed) in ((wq_b, qts, True), (wk_b, kts, True), (wv_b, vs, False)):
                for a in range(DT if transposed else NT):
                    for g in range(4 if transposed else 2):
                        bank = psC if si % 2 == 0 else psD
                        em.w("tensor", c_v, x_tr_done)
                        if len(evacs) >= 2:
                            em.w("tensor", c_v, evacs[-2])
                        for dk in range(DT):
                            if transposed:
                                mm = em.op("tensor", lambda En, bank=bank, wgt=wgt, dk=dk, a=a, g=g: En.matmul(
                                    bank[:, :], wgt[:, dk * D + a * 128: dk * D + (a + 1) * 128],
                                    xt_b[:, dk * N + g * 512: dk * N + (g + 1) * 512],
                                    start=(dk == 0), stop=(dk == DT - 1)), inc=c_t)
                            else:
                                mm = em.op("tensor", lambda En, bank=bank, wgt=wgt, dk=dk, a=a, g=g: En.matmul(
                                    bank[:, :], xt_b[:, dk * N + a * 128: dk * N + (a + 1) * 128],
                                    wgt[:, dk * D + g * 512: dk * D + (g + 1) * 512],
                                    start=(dk == 0), stop=(dk == DT - 1)), inc=c_t)
                        proj_last_t = mm
                        sslot = si % 4
                        em.w("vector", c_t, mm)
                        em.w("vector", c_st, stg_free[sslot])
                        ev = em.op("vector", lambda En, bank=bank, sslot=sslot: En.tensor_copy(
                            stg[:, sslot * 512:(sslot + 1) * 512], bank[:, :]), inc=c_v)
                        evacs.append(ev)
                        em.w("gpsimd", c_v, ev)
                        if transposed:
                            off = seq * D * N + a * 128 * N + g * 512
                            dims = [[N, 128], [1, 512]]
                        else:
                            off = seq * N * D + a * 128 * D + g * 512
                            dims = [[D, 128], [1, 512]]
                        stv = em.op("gpsimd", lambda En, dst=dst, off=off, dims=dims, sslot=sslot: En.dma_start(
                            out=dap(dst, off, dims), in_=stg[:, sslot * 512:(sslot + 1) * 512]), inc=c_st, k=16)
                        stg_free[sslot] = stv
                        si += 1
            p1_last_evac = evacs[-1]
            p1_seq_st.append(c_st.n)
            em.flush()
    p1_done_v = c_v.n
    es_p1.close()

    NB = 19
    with nc.sbuf_tensor("mstx", [128, 2816], F32) as mstx, \
         nc.sbuf_tensor("abl", [128, NB * 512], F32) as abl, \
         nc.sbuf_tensor("ktb", [64, 2 * N], BF16) as ktb, \
         nc.sbuf_tensor("qtb", [64, 2 * N], BF16) as qtb, \
         nc.sbuf_tensor("vxb", [128, 2 * NT * 65], BF16) as vxb, \
         nc.sbuf_tensor("pb", [128, 3 * 512], BF16) as pb, \
         nc.sbuf_tensor("pm", [128, 3 * 512], BF16) as pm, \
         nc.sbuf_tensor("sbs", [128, 2 * 512], F32) as sbs, \
         nc.sbuf_tensor("sbctx", [64, 512], F32) as sbctx, \
         nc.sbuf_tensor("rd", [1, 512], F32) as rd, \
         nc.sbuf_tensor("ctxn", [64, 2 * N], BF16) as ctxn:
        g_iota = em.op("gpsimd", lambda En: En.iota(
            mstx[:, :], [[1, 2816]], base=-384, channel_multiplier=-1,
            allow_small_or_imprecise_dtypes=True), inc=c_g)
        em.w("vector", c_g, g_iota)
        em.op("vector", lambda En: En.tensor_scalar_max(mstx[:, :], mstx[:, :], -30.0))
        v_mst = em.drain("vector", inc=c_v)
        v_vx1 = em.op("vector", lambda En: En.memset(
            dap(vxb, 64, [[2 * NT * 65, 128], [65, 2 * NT], [1, 1]]), 1.0), inc=c_v)

        qkv_ready = {}
        slot_free_t = [0, 0]
        ctxn_free = [0, 0]
        M = H * S

        def issue_qkv_load(m):
            h, seq = divmod(m, S)
            sl = m % 2
            em.w("sync", c_t, slot_free_t[sl])
            em.w("sync", c_v, v_vx1)
            em.w("sync", c_st, p1_seq_st[seq])
            em.op("sync", lambda En, seq=seq, h=h, sl=sl: En.dma_start(
                out=ktb[:, sl * N:(sl + 1) * N],
                in_=dap(kts, seq * D * N + h * 64 * N, [[N, 64], [1, N]])), inc=c_ld, k=16)
            em.op("sync", lambda En, seq=seq, h=h, sl=sl: En.dma_start(
                out=qtb[:, sl * N:(sl + 1) * N],
                in_=dap(qts, seq * D * N + h * 64 * N, [[N, 64], [1, N]])), inc=c_ld, k=16)
            qkv_ready[m] = em.op("sync", lambda En, seq=seq, h=h, sl=sl: En.dma_start(
                out=dap(vxb, sl * NT * 65, [[2 * NT * 65, 128], [65, NT], [1, 64]]),
                in_=dap(vs, seq * N * D + h * 64, [[D, 128], [128 * D, NT], [1, 64]])), inc=c_ld, k=16)

        issue_qkv_load(0)
        issue_qkv_load(1)

        blk_j = 0
        stt_done = {}
        exp_done = {}
        ctx_done = {}
        pm_free = {}
        ctx_bankfree_v = {0: p1_done_v, 1: p1_done_v}
        ctx_bankfree_s = {0: 0, 1: 0}
        prev_tbc = 0
        prev_vev = 0
        abl_last_user = 0
        first_p2 = True
        for h in range(H):
            em.w("scalar", c_v, v_mst)
            em.w("scalar", c_v, v_c8)
            em.w("scalar", c_v, abl_last_user)
            a_ready = 0
            for b in range(NB):
                u0 = 384 + 128 * (b - 3)
                sg = em.op("scalar", lambda En, b=b, u0=u0, h=h: En.activation(
                    abl[:, b * 512:(b + 1) * 512], mstx[:, u0:u0 + 512], AF.Sigmoid,
                    bias=nvcol[:, h:h + 1], scale=wcol[:, h:h + 1]), inc=c_s)
                em.w("vector", c_s, sg)
                a_ready = em.op("vector", lambda En, b=b, h=h: En.tensor_scalar_mul(
                    abl[:, b * 512:(b + 1) * 512], abl[:, b * 512:(b + 1) * 512],
                    c8col[:, h:h + 1]), inc=c_v)

            for seq in range(S):
                m = h * S + seq
                sl = m % 2
                em.w("tensor", c_ld, qkv_ready[m])
                if first_p2:
                    em.w("tensor", c_v, p1_done_v)
                    first_p2 = False
                cb = m % 2
                last_t_this = 0
                v_ev = 0
                for qg in range(4):
                    cxid = qg % 2
                    cxbank = psC if cxid == 0 else psD
                    nkt = 4 * qg + 4
                    for kt in range(nkt):
                        beta = 4 * qg - kt
                        bi = beta + 3
                        partial = beta <= 0
                        jj = blk_j
                        sb_sl = jj % 2
                        p_sl = jj % 3
                        sbank = psA if jj % 2 == 0 else psB
                        em.w("tensor", c_v, stt_done.get(jj - 2, 0))
                        mm_s = em.op("tensor", lambda En, sbank=sbank, sl=sl, kt=kt, qg=qg: En.matmul(
                            sbank[:, :], ktb[:, sl * N + kt * 128: sl * N + (kt + 1) * 128],
                            qtb[:, sl * N + qg * 512: sl * N + (qg + 1) * 512],
                            start=True, stop=True), inc=c_t)
                        em.w("vector", c_t, mm_s)
                        em.w("vector", c_v, a_ready)
                        em.w("vector", c_s, exp_done.get(jj - 2, 0))
                        vst = em.op("vector", lambda En, sbank=sbank, sb_sl=sb_sl, bi=bi: En.scalar_tensor_tensor(
                            out=sbs[:, sb_sl * 512:(sb_sl + 1) * 512], in0=sbank[:, :],
                            scalar=0.0, in1=abl[:, bi * 512:(bi + 1) * 512],
                            op0=ALU.max, op1=ALU.mult), inc=c_v)
                        stt_done[jj] = vst
                        em.w("scalar", c_v, vst)
                        em.w("scalar", c_t, ctx_done.get(jj - 3, 0))
                        sxp = em.op("scalar", lambda En, sb_sl=sb_sl, p_sl=p_sl: En.activation(
                            pb[:, p_sl * 512:(p_sl + 1) * 512], sbs[:, sb_sl * 512:(sb_sl + 1) * 512],
                            AF.Exp), inc=c_s)
                        exp_done[jj] = sxp
                        rhs_buf = pb
                        if partial:
                            em.w("gpsimd", c_s, sxp)
                            em.w("gpsimd", c_t, pm_free.get(p_sl, 0))
                            gsl = em.op("gpsimd", lambda En, p_sl=p_sl, beta=beta: En.affine_select(
                                pm[:, p_sl * 512:(p_sl + 1) * 512], pb[:, p_sl * 512:(p_sl + 1) * 512],
                                [[1, 512]], ALU.is_ge, 0.0, base=128 * beta,
                                channel_multiplier=-1), inc=c_g)
                            rhs_buf = pm
                        if partial:
                            em.w("tensor", c_g, gsl)
                        else:
                            em.w("tensor", c_s, sxp)
                        if kt == 0:
                            em.w("tensor", c_v, ctx_bankfree_v[cxid])
                            em.w("tensor", c_s, ctx_bankfree_s[cxid])
                        mm_c = em.op("tensor", lambda En, cxbank=cxbank, sl=sl, kt=kt, rhs_buf=rhs_buf, p_sl=p_sl, nkt=nkt: En.matmul(
                            cxbank[0:65, :], vxb[:, sl * NT * 65 + kt * 65: sl * NT * 65 + (kt + 1) * 65],
                            rhs_buf[:, p_sl * 512:(p_sl + 1) * 512],
                            start=(kt == 0), stop=(kt == nkt - 1)), inc=c_t)
                        ctx_done[jj] = mm_c
                        if partial:
                            pm_free[p_sl] = mm_c
                        last_t_this = mm_c
                        blk_j += 1
                    em.w("vector", c_t, ctx_done[blk_j - 1])
                    em.w("vector", c_t, prev_tbc)
                    em.op("vector", lambda En, cxbank=cxbank: En.reciprocal(rd[:, :], cxbank[64:65, :]))
                    v_rd = em.drain("vector", inc=c_v)
                    em.w("tensor", c_v, v_rd)
                    em.w("tensor", c_v, prev_vev)
                    prev_tbc = em.op("tensor", lambda En: En.matmul(
                        psE2[0:64, :], ones_f[:, 0:64], rd[:, :], start=True, stop=True), inc=c_t)
                    em.w("scalar", c_t, ctx_done[blk_j - 1])
                    em.w("scalar", c_v, prev_vev)
                    s_cx = em.op("scalar", lambda En, cxbank=cxbank: En.copy(sbctx[:, :], cxbank[0:64, :]), inc=c_s)
                    em.w("vector", c_s, s_cx)
                    em.w("vector", c_t, prev_tbc)
                    em.w("vector", c_st, ctxn_free[cb])
                    v_ev = em.op("vector", lambda En, cb=cb, qg=qg: En.tensor_tensor(
                        ctxn[:, cb * N + qg * 512: cb * N + (qg + 1) * 512],
                        sbctx[:, :], psE2[0:64, :], op=ALU.mult), inc=c_v)
                    prev_vev = v_ev
                    ctx_bankfree_v[cxid] = v_rd
                    ctx_bankfree_s[cxid] = s_cx
                slot_free_t[sl] = last_t_this
                if m + 2 < M:
                    issue_qkv_load(m + 2)
                abl_last_user = stt_done[blk_j - 1]
                em.w("gpsimd", c_v, v_ev)
                ctxn_free[cb] = em.op("gpsimd", lambda En, seq=seq, h=h, cb=cb: En.dma_start(
                    out=dap(cts, seq * D * N + h * 64 * N, [[N, 64], [1, N]]),
                    in_=ctxn[0:64, cb * N:(cb + 1) * N]), inc=c_st, k=16)
            em.flush()
    p2_done_st = c_st.n
    p2_done_v = c_v.n
    p2_done_s = c_s.n

    with nc.sbuf_tensor("acc", [128, NT * 1024], F32) as acc, \
         nc.sbuf_tensor("y1t", [128, DT * N], BF16) as y1t, \
         nc.sbuf_tensor("arena", [128, 16384], BF16) as arena, \
         nc.sbuf_tensor("w1q", [128, 8192], BF16) as w1q, \
         nc.sbuf_tensor("w2q", [128, 8192], BF16) as w2q, \
         nc.sbuf_tensor("xres", [128, 2 * 1024], F32) as xres, \
         nc.sbuf_tensor("y1stg", [128, 1024], BF16) as y1stg, \
         nc.sbuf_tensor("logits", [128, NT * E], F32) as logits, \
         nc.sbuf_tensor("masks", [128, NT * E], F32) as masks, \
         nc.sbuf_tensor("mkt", [8, 128], F32) as mkt, \
         nc.sbuf_tensor("gwb", [128, D], F32) as gwb, \
         nc.sbuf_tensor("gsc", [128, D], F32) as gsc, \
         nc.sbuf_tensor("stats", [128, 8 * NT], F32) as stats, \
         nc.sbuf_tensor("cols", [128, 8], F32) as cols:
        SUM0, SUM1, SSQ0, SSQ1, MEAN, RSTD, TMP, NF = range(8)

        def stat(k, qt):
            return stats[:, k * NT + qt: k * NT + qt + 1]

        def statblk(k):
            return stats[:, k * NT:(k + 1) * NT]

        w1_ready = 0
        w2_ready = 0
        m1_done_prev = 0
        m2_done_prev = 0

        def load_w1(e, fq):
            nonlocal w1_ready
            em.w("sync", c_t, m1_done_prev)
            w1_ready = em.op("sync", lambda En, e=e, fq=fq: En.dma_start(
                out=w1q[:, :],
                in_=dap(w1b, e * D * DFF + fq * 1024, [[DFF, 128], [128 * DFF, DT], [1, 1024]])), inc=c_ld, k=16)

        def load_w2(e, fq):
            nonlocal w2_ready
            em.w("sync", c_t, m2_done_prev)
            w2_ready = em.op("sync", lambda En, e=e, fq=fq: En.dma_start(
                out=w2q[:, :],
                in_=dap(w2b, e * DFF * D + fq * 1024 * D, [[D, 128], [128 * D, 8], [1, 1024]])), inc=c_ld, k=16)

        load_w1(0, 0)
        load_w2(0, 0)

        arena_last_use = 0
        y1t_last_use = 0
        gwb_used = 0
        prev_out_last = 0
        for seq in range(S):
            em.w("sync", c_st, p2_done_st)
            if seq > 0:
                em.w("sync", c_t, arena_last_use)
            lctx = em.op("sync", lambda En, seq=seq: En.dma_start(
                out=arena[:, :],
                in_=dap(cts, seq * D * N, [[N, 128], [128 * N, DT], [1, N]])), inc=c_ld, k=16)
            xres_free = [0, 0]
            wo_evacs = {}
            mm = 0
            for qt in range(NT):
                xsl = qt % 2
                em.w("sync", c_v, xres_free[xsl])
                lxr = em.op("sync", lambda En, seq=seq, qt=qt, xsl=xsl: En.dma_start(
                    out=xres[:, xsl * 1024:(xsl + 1) * 1024],
                    in_=dap(xin, seq * N * D + qt * 128 * D, [[D, 128], [1, D]])), inc=c_ld, k=16)
                for ch in range(2):
                    bank = psA if ch == 0 else psB
                    em.w("tensor", c_ld, lctx)
                    em.w("tensor", c_ld, prepass_ld)
                    em.w("tensor", c_v, prepass_v)
                    em.w("tensor", c_v, p2_done_v)
                    em.w("tensor", c_s, p2_done_s)
                    if seq > 0 and qt == 0:
                        em.w("tensor", c_st, prev_out_last)
                    em.w("tensor", c_v, wo_evacs.get((qt - 1, ch), 0))
                    for dk in range(DT):
                        mm = em.op("tensor", lambda En, bank=bank, dk=dk, qt=qt, ch=ch: En.matmul(
                            bank[:, :], arena[:, dk * N + qt * 128: dk * N + (qt + 1) * 128],
                            wo_b[:, dk * D + ch * 512: dk * D + (ch + 1) * 512],
                            start=(dk == 0), stop=(dk == DT - 1)), inc=c_t)
                    em.w("vector", c_t, mm)
                    em.w("vector", c_ld, lxr)
                    if seq > 0 and qt == 0 and ch == 0:
                        em.w("vector", c_st, prev_out_last)
                    em.op("vector", lambda En, bank=bank, qt=qt, ch=ch: En.scalar_tensor_tensor(
                        out=acc[:, qt * 1024 + ch * 512: qt * 1024 + (ch + 1) * 512],
                        in0=bank[:, :], scalar=1.0, in1=bobc[:, ch * 512:(ch + 1) * 512],
                        op0=ALU.mult, op1=ALU.add), inc=c_v)
                    em.op("vector", lambda En, qt=qt, ch=ch, xsl=xsl: En.scalar_tensor_tensor(
                        out=acc[:, qt * 1024 + ch * 512: qt * 1024 + (ch + 1) * 512],
                        in0=acc[:, qt * 1024 + ch * 512: qt * 1024 + (ch + 1) * 512],
                        scalar=1.0, in1=xres[:, xsl * 1024 + ch * 512: xsl * 1024 + (ch + 1) * 512],
                        op0=ALU.mult, op1=ALU.add,
                        accum_out=stat(SUM0 + ch, qt)), inc=c_v)
                    ev = em.op("vector", lambda En, qt=qt, ch=ch: En.scalar_tensor_tensor(
                        out=gsc[:, ch * 512:(ch + 1) * 512],
                        in0=acc[:, qt * 1024 + ch * 512: qt * 1024 + (ch + 1) * 512],
                        scalar=1.0, in1=acc[:, qt * 1024 + ch * 512: qt * 1024 + (ch + 1) * 512],
                        op0=ALU.mult, op1=ALU.mult,
                        accum_out=stat(SSQ0 + ch, qt)), inc=c_v)
                    wo_evacs[(qt, ch)] = ev
                xres_free[xsl] = wo_evacs[(qt, 1)]
            em.w("vector", c_v, wo_evacs[(NT - 1, 1)])
            em.drain("vector")
            em.op("vector", lambda En: En.tensor_tensor(statblk(MEAN), statblk(SUM0), statblk(SUM1), op=ALU.add))
            em.drain("vector")
            em.op("vector", lambda En: En.tensor_scalar_mul(statblk(MEAN), statblk(MEAN), 1.0 / D))
            em.drain("vector")
            em.op("vector", lambda En: En.tensor_tensor(statblk(TMP), statblk(SSQ0), statblk(SSQ1), op=ALU.add))
            em.drain("vector")
            em.op("vector", lambda En: En.tensor_scalar_mul(statblk(TMP), statblk(TMP), 1.0 / D))
            em.drain("vector")
            em.op("vector", lambda En: En.tensor_tensor(statblk(RSTD), statblk(MEAN), statblk(MEAN), op=ALU.mult))
            em.drain("vector")
            em.op("vector", lambda En: En.tensor_tensor(statblk(TMP), statblk(TMP), statblk(RSTD), op=ALU.subtract))
            v_var = em.drain("vector", inc=c_v)
            em.w("scalar", c_v, v_var)
            s_sd = em.op("scalar", lambda En: En.activation(statblk(RSTD), statblk(TMP), AF.Sqrt, bias=epsc[:, 0:1]), inc=c_s)
            em.w("vector", c_s, s_sd)
            em.op("vector", lambda En: En.reciprocal(statblk(RSTD), statblk(RSTD)))
            v_rstd = em.drain("vector", inc=c_v)
            y1_tr = {}
            cst = 0
            for qt in range(NT):
                em.w("vector", c_v, v_rstd)
                nrm = em.op("vector", lambda En, qt=qt: En.tensor_scalar(
                    out=acc[:, qt * 1024:(qt + 1) * 1024], in0=acc[:, qt * 1024:(qt + 1) * 1024],
                    scalar1=stat(MEAN, qt), scalar2=stat(RSTD, qt),
                    op0=ALU.subtract, op1=ALU.mult), inc=c_v)
                em.w("scalar", c_v, nrm)
                em.w("scalar", c_v, y1_tr.get(qt - 1, 0))
                cst = em.op("scalar", lambda En, qt=qt: En.copy(y1stg[:, :], acc[:, qt * 1024:(qt + 1) * 1024]), inc=c_s)
                em.w("tensor", c_s, cst)
                if qt >= 1:
                    em.w("tensor", c_v, y1_tr[qt - 1])
                if seq > 0 and qt == 0:
                    em.w("tensor", c_t, y1t_last_use)
                ttr = 0
                for dt in range(DT):
                    ttr = em.op("tensor", lambda En, dt=dt: En.transpose(
                        psT[:, dt * 128:(dt + 1) * 128], y1stg[:, dt * 128:(dt + 1) * 128],
                        eye_b[:, :]), inc=c_t)
                em.w("vector", c_t, ttr)
                if seq > 0 and qt == 0:
                    em.w("vector", c_t, y1t_last_use)
                y1_tr[qt] = em.op("vector", lambda En, qt=qt: En.tensor_copy(
                    dap(y1t, qt * 128, [[DT * N, 128], [N, DT], [1, 128]]), psT[:, :]), inc=c_v)
            y1t_ready = y1_tr[NT - 1]
            lg = 0
            gw_row_reads = 0
            for e in range(E):
                em.w("sync", c_v, wo_evacs[(NT - 1, 1)])
                em.w("sync", c_t, gw_row_reads)
                lgw = em.op("sync", lambda En, e=e: En.dma_start(
                    out=xres[0:1, 0:1024], in_=dap(gw, e * D, [[D, 1], [1, D]])), inc=c_ld, k=16)
                em.w("tensor", c_ld, lgw)
                em.w("tensor", c_v, prepass_v)
                if gwb_used:
                    em.w("tensor", c_v, gwb_used)
                em.op("tensor", lambda En, e=e: En.matmul(
                    psE2[:, :], ones_f[:, :], xres[0:1, 0:512], start=True, stop=True), inc=c_t)
                t_g2 = em.op("tensor", lambda En, e=e: En.matmul(
                    psD[:, :], ones_f[:, :], xres[0:1, 512:1024],
                    start=True, stop=True), inc=c_t)
                gw_row_reads = t_g2
                em.w("vector", c_t, t_g2)
                em.op("vector", lambda En: En.tensor_copy(gwb[:, 0:512], psE2[:, 0:512]), inc=c_v)
                v_gw = em.op("vector", lambda En: En.tensor_copy(gwb[:, 512:1024], psD[:, 0:512]), inc=c_v)
                for qt in range(NT):
                    lg = em.op("vector", lambda En, qt=qt, e=e: En.scalar_tensor_tensor(
                        out=gsc[:, :], in0=acc[:, qt * 1024:(qt + 1) * 1024], scalar=1.0,
                        in1=gwb[:, :], op0=ALU.mult, op1=ALU.mult,
                        accum_out=logits[:, qt * E + e: qt * E + e + 1]), inc=c_v)
                gwb_used = lg
            logits_done = lg
            em.drain("vector")
            mask_v = {}
            for qt in range(NT):
                em.op("vector", lambda En, qt=qt: En.tensor_tensor(
                    logits[:, qt * E:(qt + 1) * E], logits[:, qt * E:(qt + 1) * E], gbb[:, :], op=ALU.add))
                em.drain("vector")
                em.op("vector", lambda En, qt=qt: En.reduce_max(cols[:, 0:1], logits[:, qt * E:(qt + 1) * E], axis=AX.X))
                em.drain("vector")
                em.op("vector", lambda En: En.memset(cols[:, 1:2], 1.0))
                em.drain("vector")
                for e in range(E):
                    em.op("vector", lambda En, qt=qt, e=e: En.tensor_tensor(
                        cols[:, 2:3], logits[:, qt * E + e: qt * E + e + 1], cols[:, 0:1], op=ALU.is_ge))
                    em.drain("vector")
                    em.op("vector", lambda En, qt=qt, e=e: En.tensor_tensor(
                        masks[:, qt * E + e: qt * E + e + 1], cols[:, 2:3], cols[:, 1:2], op=ALU.mult))
                    em.drain("vector")
                    em.op("vector", lambda En, qt=qt, e=e: En.tensor_tensor(
                        cols[:, 1:2], cols[:, 1:2], masks[:, qt * E + e: qt * E + e + 1], op=ALU.subtract))
                    em.drain("vector")
                mask_v[qt] = em.drain("vector", inc=c_v)
            b2_evac = 0
            b2add = 0
            mkt_reads = 0
            for qt in range(NT):
                em.w("tensor", c_v, mask_v[qt])
                em.w("tensor", c_v, b2_evac)
                tmk = em.op("tensor", lambda En, qt=qt: En.transpose(
                    psC[0:8, 0:128], masks[:, qt * E:(qt + 1) * E], eye_f[:, :]), inc=c_t)
                em.w("vector", c_t, tmk)
                em.w("vector", c_t, mkt_reads)
                b2_evac = em.op("vector", lambda En, qt=qt: En.tensor_copy(
                    mkt[:, :], psC[0:8, 0:128]), inc=c_v)
                for ch in range(2):
                    em.w("tensor", c_v, b2_evac)
                    if b2add:
                        em.w("tensor", c_v, b2add)
                    tb2 = em.op("tensor", lambda En, qt=qt, ch=ch: En.matmul(
                        psD[:, :], mkt[:, :], b2s[:, ch * 512:(ch + 1) * 512],
                        start=True, stop=True), inc=c_t)
                    mkt_reads = tb2
                    em.w("vector", c_t, tb2)
                    b2add = em.op("vector", lambda En, qt=qt, ch=ch: En.scalar_tensor_tensor(
                        out=acc[:, qt * 1024 + ch * 512: qt * 1024 + (ch + 1) * 512],
                        in0=psD[:, :], scalar=1.0,
                        in1=acc[:, qt * 1024 + ch * 512: qt * 1024 + (ch + 1) * 512],
                        op0=ALU.mult, op1=ALU.add), inc=c_v)
            acc_base = b2add
            em.flush()

            h_evac = {}
            acc_upd = {}
            for e in range(E):
                for fq in range(4):
                    w1_now, w2_now = w1_ready, w2_ready
                    m1_last = 0
                    for ft in range(8):
                        for tb in range(4):
                            idx = ft * 4 + tb
                            bank = psA if idx % 2 == 0 else psB
                            em.w("tensor", c_ld, w1_now)
                            em.w("tensor", c_v, y1t_ready)
                            if e == 0 and fq == 0:
                                em.w("tensor", c_v, acc_base)
                            em.w("tensor", c_s, h_evac.get(idx - 2, 0))
                            for dk in range(DT):
                                mmx = em.op("tensor", lambda En, bank=bank, ft=ft, tb=tb, dk=dk: En.matmul(
                                    bank[:, :], w1q[:, dk * 1024 + ft * 128: dk * 1024 + (ft + 1) * 128],
                                    y1t[:, dk * N + tb * 512: dk * N + (tb + 1) * 512],
                                    start=(dk == 0), stop=(dk == DT - 1)), inc=c_t)
                            m1_last = mmx
                            em.w("scalar", c_t, mmx)
                            em.w("scalar", c_t, m2_done_prev if idx == 0 else 0)
                            h_evac[idx] = em.op("scalar", lambda En, bank=bank, ft=ft, tb=tb, e=e, fq=fq: En.activation(
                                arena[:, ft * N + tb * 512: ft * N + (tb + 1) * 512], bank[:, :],
                                AF.Relu, bias=b1s[:, (fq * 8 + ft) * E + e: (fq * 8 + ft) * E + e + 1]), inc=c_s)
                    m1_done_prev = m1_last
                    h_ready = h_evac[31]
                    nxt = e * 4 + fq + 1
                    if nxt < 32:
                        load_w1(nxt // 4, nxt % 4)
                    elif seq < S - 1:
                        load_w1(0, 0)
                    m2_last = 0
                    for tt in range(NT):
                        for ch in range(2):
                            idx = tt * 2 + ch
                            bank = psC if idx % 2 == 0 else psD
                            em.w("tensor", c_ld, w2_now)
                            em.w("tensor", c_s, h_ready)
                            em.w("tensor", c_v, acc_upd.get(idx - 2, 0))
                            for ft in range(8):
                                mmx = em.op("tensor", lambda En, bank=bank, tt=tt, ch=ch, ft=ft: En.matmul(
                                    bank[:, :], arena[:, ft * N + tt * 128: ft * N + (tt + 1) * 128],
                                    w2q[:, ft * 1024 + ch * 512: ft * 1024 + (ch + 1) * 512],
                                    start=(ft == 0), stop=(ft == 7)), inc=c_t)
                            m2_last = mmx
                            em.w("vector", c_t, mmx)
                            em.w("vector", c_v, acc_base)
                            acc_upd[idx] = em.op("vector", lambda En, bank=bank, tt=tt, ch=ch, e=e: En.scalar_tensor_tensor(
                                out=acc[:, tt * 1024 + ch * 512: tt * 1024 + (ch + 1) * 512],
                                in0=bank[:, :], scalar=masks[:, tt * E + e: tt * E + e + 1],
                                in1=acc[:, tt * 1024 + ch * 512: tt * 1024 + (ch + 1) * 512],
                                op0=ALU.mult, op1=ALU.add), inc=c_v)
                    m2_done_prev = m2_last
                    if nxt < 32:
                        load_w2(nxt // 4, nxt % 4)
                    elif seq < S - 1:
                        load_w2(0, 0)
                em.flush()
            moe_done_v = acc_upd[NT * 2 - 1]
            y1t_last_use = m1_done_prev
            arena_last_use = m2_done_prev

            for qt in range(NT):
                em.w("vector", c_v, moe_done_v)
                em.op("vector", lambda En, qt=qt: En.tensor_scalar(
                    out=gsc[:, :], in0=acc[:, qt * 1024:(qt + 1) * 1024], scalar1=1.0, scalar2=0.0,
                    op0=ALU.mult, op1=ALU.add, accum_out=stat(SUM0, qt)), inc=c_v)
                em.op("vector", lambda En, qt=qt: En.scalar_tensor_tensor(
                    out=gsc[:, :], in0=acc[:, qt * 1024:(qt + 1) * 1024], scalar=1.0,
                    in1=acc[:, qt * 1024:(qt + 1) * 1024], op0=ALU.mult, op1=ALU.mult,
                    accum_out=stat(SSQ0, qt)), inc=c_v)
            em.drain("vector")
            em.op("vector", lambda En: En.tensor_scalar_mul(statblk(MEAN), statblk(SUM0), 1.0 / D))
            em.drain("vector")
            em.op("vector", lambda En: En.tensor_scalar_mul(statblk(TMP), statblk(SSQ0), 1.0 / D))
            em.drain("vector")
            em.op("vector", lambda En: En.tensor_tensor(statblk(RSTD), statblk(MEAN), statblk(MEAN), op=ALU.mult))
            em.drain("vector")
            em.op("vector", lambda En: En.tensor_tensor(statblk(TMP), statblk(TMP), statblk(RSTD), op=ALU.subtract))
            v_var2 = em.drain("vector", inc=c_v)
            em.w("scalar", c_v, v_var2)
            s_sd2 = em.op("scalar", lambda En: En.activation(statblk(RSTD), statblk(TMP), AF.Sqrt, bias=epsc[:, 0:1]), inc=c_s)
            em.w("vector", c_s, s_sd2)
            em.op("vector", lambda En: En.reciprocal(statblk(RSTD), statblk(RSTD)))
            v_rstd2 = em.drain("vector", inc=c_v)
            for qt in range(NT):
                em.w("vector", c_v, v_rstd2)
                nout = em.op("vector", lambda En, qt=qt: En.tensor_scalar(
                    out=acc[:, qt * 1024:(qt + 1) * 1024], in0=acc[:, qt * 1024:(qt + 1) * 1024],
                    scalar1=stat(MEAN, qt), scalar2=stat(RSTD, qt),
                    op0=ALU.subtract, op1=ALU.mult), inc=c_v)
                em.w("gpsimd", c_v, nout)
                prev_out_last = em.op("gpsimd", lambda En, seq=seq, qt=qt: En.dma_start(
                    out=dap(yout, seq * N * D + qt * 128 * D, [[D, 128], [1, D]]),
                    in_=acc[:, qt * 1024:(qt + 1) * 1024]), inc=c_st, k=16)
            em.flush()

        em.w("gpsimd", c_st, c_st.n)
        em.flush()
    es.close()
    return nc


_CACHE = {}


def kernel(L0_attn_x, L0_attn_Wq, L0_attn_Wk, L0_attn_Wv, L0_attn_Wo, L0_attn_bo,
           L0_attn_w, L0_attn_v, L0_ffn_gate_w, L0_ffn_gate_b, L0_ffn_expert_bias,
           L0_ffn_w1, L0_ffn_b1, L0_ffn_w2, L0_ffn_b2):
    if "nc" not in _CACHE:
        _CACHE["nc"] = build()
    nc = _CACHE["nc"]

    def f32(a):
        return np.ascontiguousarray(np.asarray(a, dtype=np.float32))

    x = f32(L0_attn_x)
    b1 = f32(L0_ffn_b1)
    b1rr = np.ascontiguousarray(b1.T.reshape(FT, 128, E).transpose(1, 0, 2).reshape(128, FT * E))
    common = {
        "wq": f32(L0_attn_Wq), "wk": f32(L0_attn_Wk), "wv": f32(L0_attn_Wv),
        "wo": f32(L0_attn_Wo), "bo": f32(L0_attn_bo).reshape(1, D),
        "wdec": f32(L0_attn_w).reshape(1, H), "vdec": f32(L0_attn_v).reshape(1, H),
        "gw": f32(L0_ffn_gate_w), "gb": f32(L0_ffn_gate_b).reshape(1, E),
        "ebias": f32(L0_ffn_expert_bias).reshape(1, E),
        "w1": f32(L0_ffn_w1), "b1r": b1rr, "w2": f32(L0_ffn_w2), "b2": f32(L0_ffn_b2),
        "eye": np.eye(128, dtype=np.float32),
        "onesr": np.ones((1, 128), dtype=np.float32),
    }
    in_maps = []
    for c in range(8):
        m = dict(common)
        m["x"] = np.ascontiguousarray(x[c * S:(c + 1) * S])
        in_maps.append(m)
    res = run_bass_kernel_spmd(nc, in_maps, core_ids=list(range(8)))
    out = np.concatenate([np.asarray(r["y"], dtype=np.float32) for r in res.results], axis=0)
    return out.reshape(32, N, D)


# version 06
# speedup vs baseline: 1.1290x; 1.1071x over previous
import numpy as np
from contextlib import ExitStack

import concourse.bass as bass
import concourse.mybir as mybir
from concourse.bass_utils import run_bass_kernel_spmd

F32 = mybir.dt.float32
BF16 = mybir.dt.bfloat16
AF = mybir.ActivationFunctionType
ALU = mybir.AluOpType
AX = mybir.AxisListType

S = 4
N = 2048
D = 1024
H = 16
E = 8
DFF = 4096
NT = N // 128
DT = D // 128
FT = DFF // 128

ENGINES = ("sync", "tensor", "vector", "scalar", "gpsimd")


class Ctr:
    def __init__(self, nc, es, name):
        self.h = es.enter_context(nc.semaphore(name))
        self.name = name
        self.n = 0

    def bump(self, k=1):
        self.n += k
        return self.n


class EM:
    def __init__(self, nc):
        self.nc = nc
        self.q = {e: [] for e in ENGINES}
        self.lastwait = {}

    def w(self, eng, ctr, val):
        if val is None or val <= 0:
            return
        assert val <= ctr.n, f"wait on future count {ctr.name} {val} > {ctr.n}"
        key = (eng, ctr.name)
        if self.lastwait.get(key, 0) >= val:
            return
        self.lastwait[key] = val
        self.q[eng].append(lambda Eng, h=ctr.h, v=val: Eng.wait_ge(h, v))

    def op(self, eng, fn, inc=None, k=1):
        if inc is not None:
            val = inc.bump(k)

            def emit(Eng, fn=fn, h=inc.h, k=k):
                fn(Eng).then_inc(h, k)
            self.q[eng].append(emit)
            return val
        self.q[eng].append(lambda Eng, fn=fn: fn(Eng))
        return None

    def drain(self, eng, inc=None):
        return self.op(eng, lambda Eng: Eng.drain(), inc=inc)

    def flush(self):
        qs = self.q
        self.q = {e: [] for e in ENGINES}
        if not any(qs.values()):
            return
        with self.nc.Block() as blk:
            for e in ENGINES:
                fns = qs[e]
                if fns:
                    getattr(blk, e)(lambda Eng, fns=fns: [f(Eng) for f in fns])


def build():
    nc = bass.Bass("TRN2", target_bir_lowering=False)

    xin = nc.dram_tensor("x", [S, N, D], F32, kind="ExternalInput")
    wq = nc.dram_tensor("wq", [D, D], F32, kind="ExternalInput")
    wk = nc.dram_tensor("wk", [D, D], F32, kind="ExternalInput")
    wv = nc.dram_tensor("wv", [D, D], F32, kind="ExternalInput")
    wo = nc.dram_tensor("wo", [D, D], F32, kind="ExternalInput")
    bo = nc.dram_tensor("bo", [1, D], F32, kind="ExternalInput")
    wdec = nc.dram_tensor("wdec", [1, H], F32, kind="ExternalInput")
    vdec = nc.dram_tensor("vdec", [1, H], F32, kind="ExternalInput")
    gw = nc.dram_tensor("gw", [E, D], F32, kind="ExternalInput")
    gb = nc.dram_tensor("gb", [1, E], F32, kind="ExternalInput")
    ebias = nc.dram_tensor("ebias", [1, E], F32, kind="ExternalInput")
    w1 = nc.dram_tensor("w1", [E, D, DFF], F32, kind="ExternalInput")
    b1r = nc.dram_tensor("b1r", [128, FT * E], F32, kind="ExternalInput")
    w2 = nc.dram_tensor("w2", [E, DFF, D], F32, kind="ExternalInput")
    b2 = nc.dram_tensor("b2", [E, D], F32, kind="ExternalInput")
    eye = nc.dram_tensor("eye", [128, 128], F32, kind="ExternalInput")
    onesr = nc.dram_tensor("onesr", [1, 128], F32, kind="ExternalInput")
    yout = nc.dram_tensor("y", [S, N, D], F32, kind="ExternalOutput")

    w1b = nc.dram_tensor("w1b", [E, D, DFF], BF16)
    w2b = nc.dram_tensor("w2b", [E, DFF, D], BF16)
    qts = nc.dram_tensor("qts", [S, D, N], BF16)
    kts = nc.dram_tensor("kts", [S, D, N], BF16)
    vs = nc.dram_tensor("vs", [S, N, D], BF16)
    cts = nc.dram_tensor("cts", [S, D, N], BF16)

    def dap(t, off, dims):
        return bass.AP(t, off, dims)

    es = ExitStack()
    em = EM(nc)
    c_ld = Ctr(nc, es, "c_ld")
    c_st = Ctr(nc, es, "c_st")
    c_t = Ctr(nc, es, "c_t")
    c_v = Ctr(nc, es, "c_v")
    c_s = Ctr(nc, es, "c_s")
    c_g = Ctr(nc, es, "c_g")

    eye_f = es.enter_context(nc.sbuf_tensor("eye_f", [128, 128], F32))
    eye_b = es.enter_context(nc.sbuf_tensor("eye_b", [128, 128], BF16))
    ones_f = es.enter_context(nc.sbuf_tensor("ones_f", [1, 128], F32))
    rowbuf = es.enter_context(nc.sbuf_tensor("rowbuf", [1, 2 * H], F32))
    grow = es.enter_context(nc.sbuf_tensor("grow", [1, 2 * E], F32))
    bobc = es.enter_context(nc.sbuf_tensor("bobc", [128, D], F32))
    gbb = es.enter_context(nc.sbuf_tensor("gbb", [128, E], F32))
    wcol = es.enter_context(nc.sbuf_tensor("wcol", [128, H], F32))
    nvcol = es.enter_context(nc.sbuf_tensor("nvcol", [128, H], F32))
    c8col = es.enter_context(nc.sbuf_tensor("c8col", [128, H], F32))
    b1s = es.enter_context(nc.sbuf_tensor("b1s", [128, FT * E], F32))
    b2s = es.enter_context(nc.sbuf_tensor("b2s", [E, D], F32))
    epsc = es.enter_context(nc.sbuf_tensor("epsc", [128, 1], F32))

    psA = es.enter_context(nc.psum_tensor("psA", [128, 512], F32))
    psB = es.enter_context(nc.psum_tensor("psB", [128, 512], F32))
    psC = es.enter_context(nc.psum_tensor("psC", [128, 512], F32))
    psD = es.enter_context(nc.psum_tensor("psD", [128, 512], F32))
    psE2 = es.enter_context(nc.psum_tensor("psE2", [128, 512], F32))
    psF = es.enter_context(nc.psum_tensor("psF", [128, 512], F32))
    psG = es.enter_context(nc.psum_tensor("psG", [128, 512], F32))
    psT = es.enter_context(nc.psum_tensor("psT", [128, 1024], BF16))

    em.op("sync", lambda En: En.dma_start(out=eye_f[:, :], in_=eye.ap()), inc=c_ld, k=16)
    em.op("sync", lambda En: En.dma_start(out=ones_f[:, :], in_=onesr.ap()), inc=c_ld, k=16)
    em.op("sync", lambda En: En.dma_start(out=rowbuf[:, 0:H], in_=wdec.ap()), inc=c_ld, k=16)
    em.op("sync", lambda En: En.dma_start(out=rowbuf[:, H:2 * H], in_=vdec.ap()), inc=c_ld, k=16)
    em.op("sync", lambda En: En.dma_start(out=grow[:, 0:E], in_=gb.ap()), inc=c_ld, k=16)
    em.op("sync", lambda En: En.dma_start(out=grow[:, E:2 * E], in_=ebias.ap()), inc=c_ld, k=16)
    em.op("sync", lambda En: En.dma_start(out=b1s[:, :], in_=b1r.ap()), inc=c_ld, k=16)
    ld0 = em.op("sync", lambda En: En.dma_start(out=b2s[:, :], in_=b2.ap()), inc=c_ld, k=16)

    em.w("vector", c_ld, ld0)
    em.op("vector", lambda En: En.memset(epsc[:, :], 1e-5), inc=c_v)
    em.op("vector", lambda En: En.tensor_copy(eye_b[:, :], eye_f[:, :]), inc=c_v)
    em.op("vector", lambda En: En.tensor_tensor(grow[:, 0:E], grow[:, 0:E], grow[:, E:2 * E], op=ALU.add))
    v_grow = em.drain("vector", inc=c_v)

    em.w("tensor", c_ld, ld0)
    em.w("tensor", c_v, v_grow)
    t_b1 = em.op("tensor", lambda En: En.matmul(psA[:, 0:2 * H], ones_f[:, :], rowbuf[:, :], start=True, stop=True), inc=c_t)
    t_b2 = em.op("tensor", lambda En: En.matmul(psB[:, 0:E], ones_f[:, :], grow[:, 0:E], start=True, stop=True), inc=c_t)
    em.w("vector", c_t, t_b2)
    em.op("vector", lambda En: En.tensor_copy(wcol[:, :], psA[:, 0:H]), inc=c_v)
    em.op("vector", lambda En: En.tensor_scalar_mul(nvcol[:, :], psA[:, H:2 * H], -1.0), inc=c_v)
    em.op("vector", lambda En: En.tensor_copy(gbb[:, :], psB[:, 0:E]), inc=c_v)
    em.w("scalar", c_t, t_b1)
    s_ev = em.op("scalar", lambda En: En.activation(c8col[:, :], psA[:, H:2 * H], AF.Exp), inc=c_s)
    em.w("vector", c_s, s_ev)
    em.op("vector", lambda En: En.tensor_scalar(out=c8col[:, :], in0=c8col[:, :], scalar1=1.0, scalar2=0.125, op0=ALU.add, op1=ALU.mult))
    v_c8 = em.drain("vector", inc=c_v)

    wo_b = es.enter_context(nc.sbuf_tensor("wo_b", [128, DT * D], BF16))
    es_p1 = ExitStack()
    wq_b = es_p1.enter_context(nc.sbuf_tensor("wq_b", [128, DT * D], BF16))
    wk_b = es_p1.enter_context(nc.sbuf_tensor("wk_b", [128, DT * D], BF16))
    wv_b = es_p1.enter_context(nc.sbuf_tensor("wv_b", [128, DT * D], BF16))

    with nc.sbuf_tensor("pp_f", [128, 2 * 8192], F32) as pp_f, \
         nc.sbuf_tensor("pp_b", [128, 2 * 8192], BF16) as pp_b, \
         nc.sbuf_tensor("borow", [1, D], F32) as borow:
        ldbo = em.op("sync", lambda En: En.dma_start(out=borow[:, :], in_=bo.ap()), inc=c_ld, k=16)
        em.w("tensor", c_ld, ldbo)
        em.w("tensor", c_v, v_c8)
        em.op("tensor", lambda En: En.matmul(psC[:, 0:512], ones_f[:, :], borow[:, 0:512], start=True, stop=True), inc=c_t)
        t_b3 = em.op("tensor", lambda En: En.matmul(psD[:, 0:512], ones_f[:, :], borow[:, 512:1024], start=True, stop=True), inc=c_t)
        em.w("vector", c_t, t_b3)
        em.op("vector", lambda En: En.tensor_copy(bobc[:, 0:512], psC[:, 0:512]), inc=c_v)
        em.op("vector", lambda En: En.tensor_copy(bobc[:, 512:1024], psD[:, 0:512]), inc=c_v)
        prev_cast = 0
        for (src, dstb) in ((wq, wq_b), (wk, wk_b), (wv, wv_b), (wo, wo_b)):
            em.w("sync", c_v, prev_cast)
            lv = em.op("sync", lambda En, src=src: En.dma_start(
                out=pp_f[:, 0:8192],
                in_=dap(src, 0, [[D, 128], [128 * D, DT], [1, D]])), inc=c_ld, k=16)
            em.w("vector", c_ld, lv)
            prev_cast = em.op("vector", lambda En, dstb=dstb: En.tensor_copy(dstb[:, :], pp_f[:, 0:8192]), inc=c_v)

        cast_jobs = []
        for i in range(E * D // 128):
            cast_jobs.append((w1, w1b, i * 128 * DFF, DFF))
        for i in range(E * DFF // 128):
            cast_jobs.append((w2, w2b, i * 128 * D, D))
        half = 8192
        slot_cast = [prev_cast, prev_cast]
        slot_store = [0, 0]
        for j, (src, dst, off, width) in enumerate(cast_jobs):
            sl = j % 2
            em.w("sync", c_v, slot_cast[sl])
            lv = em.op("sync", lambda En, src=src, off=off, width=width, sl=sl: En.dma_start(
                out=pp_f[:, sl * half: sl * half + width],
                in_=dap(src, off, [[width, 128], [1, width]])), inc=c_ld, k=16)
            em.w("vector", c_ld, lv)
            em.w("vector", c_st, slot_store[sl])
            cv = em.op("vector", lambda En, sl=sl, width=width: En.tensor_copy(
                pp_b[:, sl * half: sl * half + width], pp_f[:, sl * half: sl * half + width]), inc=c_v)
            slot_cast[sl] = cv
            em.w("gpsimd", c_v, cv)
            slot_store[sl] = em.op("gpsimd", lambda En, dst=dst, off=off, width=width, sl=sl: En.dma_start(
                out=dap(dst, off, [[width, 128], [1, width]]),
                in_=pp_b[:, sl * half: sl * half + width]), inc=c_st, k=16)
        em.flush()

    prepass_v = c_v.n
    prepass_s = c_s.n
    prepass_ld = c_ld.n
    prepass_st = c_st.n

    p1_seq_st = []
    proj_last_t = 0
    p1_last_evac = 0
    stg_free = [0, 0, 0, 0]
    evacs = []
    with nc.sbuf_tensor("xt_b", [128, DT * N], BF16) as xt_b, \
         nc.sbuf_tensor("stg", [128, 4 * 512], BF16) as stg:
        for seq in range(S):
            with nc.sbuf_tensor(f"xf{seq}", [128, NT * 1024], F32) as xf:
                if seq > 0:
                    em.w("sync", c_t, proj_last_t)
                lx = em.op("sync", lambda En, seq=seq: En.dma_start(
                    out=xf[:, :],
                    in_=dap(xin, seq * N * D, [[D, 128], [128 * D, NT], [1, D]])), inc=c_ld, k=16)
                tr_evac = []
                for idx in range(NT * 2):
                    tt, hb = divmod(idx, 2)
                    bank = psA if idx % 2 == 0 else psB
                    em.w("tensor", c_ld, lx)
                    em.w("tensor", c_v, prepass_v)
                    if idx >= 2:
                        em.w("tensor", c_v, tr_evac[idx - 2])
                    elif seq > 0:
                        em.w("tensor", c_v, p1_last_evac)
                    for q in range(4):
                        dt = hb * 4 + q
                        t_tr = em.op("tensor", lambda En, bank=bank, q=q, tt=tt, dt=dt: En.transpose(
                            bank[:, q * 128:(q + 1) * 128],
                            xf[:, tt * 1024 + dt * 128: tt * 1024 + (dt + 1) * 128],
                            eye_f[:, :]), inc=c_t)
                    em.w("vector", c_t, t_tr)
                    ev = em.op("vector", lambda En, bank=bank, tt=tt, hb=hb: En.tensor_copy(
                        dap(xt_b, hb * 4 * N + tt * 128, [[DT * N, 128], [N, 4], [1, 128]]),
                        bank[:, 0:512]), inc=c_v)
                    tr_evac.append(ev)
                x_tr_done = tr_evac[-1]

            si = 0
            for (wgt, dst, transposed) in ((wq_b, qts, True), (wk_b, kts, True), (wv_b, vs, False)):
                for a in range(DT if transposed else NT):
                    for g in range(4 if transposed else 2):
                        bank = psC if si % 2 == 0 else psD
                        em.w("tensor", c_v, x_tr_done)
                        if len(evacs) >= 2:
                            em.w("tensor", c_v, evacs[-2])
                        for dk in range(DT):
                            if transposed:
                                mm = em.op("tensor", lambda En, bank=bank, wgt=wgt, dk=dk, a=a, g=g: En.matmul(
                                    bank[:, :], wgt[:, dk * D + a * 128: dk * D + (a + 1) * 128],
                                    xt_b[:, dk * N + g * 512: dk * N + (g + 1) * 512],
                                    start=(dk == 0), stop=(dk == DT - 1)), inc=c_t)
                            else:
                                mm = em.op("tensor", lambda En, bank=bank, wgt=wgt, dk=dk, a=a, g=g: En.matmul(
                                    bank[:, :], xt_b[:, dk * N + a * 128: dk * N + (a + 1) * 128],
                                    wgt[:, dk * D + g * 512: dk * D + (g + 1) * 512],
                                    start=(dk == 0), stop=(dk == DT - 1)), inc=c_t)
                        proj_last_t = mm
                        sslot = si % 4
                        em.w("vector", c_t, mm)
                        em.w("vector", c_st, stg_free[sslot])
                        ev = em.op("vector", lambda En, bank=bank, sslot=sslot: En.tensor_copy(
                            stg[:, sslot * 512:(sslot + 1) * 512], bank[:, :]), inc=c_v)
                        evacs.append(ev)
                        em.w("gpsimd", c_v, ev)
                        if transposed:
                            off = seq * D * N + a * 128 * N + g * 512
                            dims = [[N, 128], [1, 512]]
                        else:
                            off = seq * N * D + a * 128 * D + g * 512
                            dims = [[D, 128], [1, 512]]
                        stv = em.op("gpsimd", lambda En, dst=dst, off=off, dims=dims, sslot=sslot: En.dma_start(
                            out=dap(dst, off, dims), in_=stg[:, sslot * 512:(sslot + 1) * 512]), inc=c_st, k=16)
                        stg_free[sslot] = stv
                        si += 1
            p1_last_evac = evacs[-1]
            p1_seq_st.append(c_st.n)
            em.flush()
    p1_done_v = c_v.n
    es_p1.close()

    NB = 19
    with nc.sbuf_tensor("mstx", [128, 2816], F32) as mstx, \
         nc.sbuf_tensor("abl", [128, NB * 512], F32) as abl, \
         nc.sbuf_tensor("ktb", [64, 2 * N], BF16) as ktb, \
         nc.sbuf_tensor("qtb", [64, 2 * N], BF16) as qtb, \
         nc.sbuf_tensor("vxb", [128, 2 * NT * 65], BF16) as vxb, \
         nc.sbuf_tensor("pb", [128, 16 * 512], BF16) as pb, \
         nc.sbuf_tensor("pm", [128, 8 * 512], BF16) as pm, \
         nc.sbuf_tensor("sbs", [128, 4 * 512], F32) as sbs, \
         nc.sbuf_tensor("sbctx", [64, 4 * 512], F32) as sbctx, \
         nc.sbuf_tensor("rd", [1, 4 * 512], F32) as rd, \
         nc.sbuf_tensor("ctxn", [64, 2 * N], BF16) as ctxn:
        g_iota = em.op("gpsimd", lambda En: En.iota(
            mstx[:, :], [[1, 2816]], base=-384, channel_multiplier=-1,
            allow_small_or_imprecise_dtypes=True), inc=c_g)
        em.w("vector", c_g, g_iota)
        em.op("vector", lambda En: En.tensor_scalar_max(mstx[:, :], mstx[:, :], -30.0))
        v_mst = em.drain("vector", inc=c_v)
        v_vx1 = em.op("vector", lambda En: En.memset(
            dap(vxb, 64, [[2 * NT * 65, 128], [65, 2 * NT], [1, 1]]), 1.0), inc=c_v)

        qkv_ready = {}
        slot_free_t = [0, 0]
        ctxn_free = [0, 0]
        M = H * S

        def issue_qkv_load(m):
            h, seq = divmod(m, S)
            sl = m % 2
            em.w("sync", c_t, slot_free_t[sl])
            em.w("sync", c_v, v_vx1)
            em.w("sync", c_st, p1_seq_st[seq])
            em.op("sync", lambda En, seq=seq, h=h, sl=sl: En.dma_start(
                out=ktb[:, sl * N:(sl + 1) * N],
                in_=dap(kts, seq * D * N + h * 64 * N, [[N, 64], [1, N]])), inc=c_ld, k=16)
            em.op("sync", lambda En, seq=seq, h=h, sl=sl: En.dma_start(
                out=qtb[:, sl * N:(sl + 1) * N],
                in_=dap(qts, seq * D * N + h * 64 * N, [[N, 64], [1, N]])), inc=c_ld, k=16)
            qkv_ready[m] = em.op("sync", lambda En, seq=seq, h=h, sl=sl: En.dma_start(
                out=dap(vxb, sl * NT * 65, [[2 * NT * 65, 128], [65, NT], [1, 64]]),
                in_=dap(vs, seq * N * D + h * 64, [[D, 128], [128 * D, NT], [1, 64]])), inc=c_ld, k=16)

        issue_qkv_load(0)
        issue_qkv_load(1)

        blk_j = 0
        stt_done = {}
        exp_done = {}
        pb_free = {}
        pm_free = {}
        bankfree_v = {0: p1_done_v, 1: p1_done_v, 2: 0, 3: 0}
        bankfree_s = {0: 0, 1: 0, 2: 0, 3: 0}
        rd_free = {}
        sbctx_free = {}
        evac_half_free = {}
        CXB = [psC, psD, psF, psG]
        abl_last_user = 0
        first_p2 = True
        for h in range(H):
            em.w("scalar", c_v, v_mst)
            em.w("scalar", c_v, v_c8)
            em.w("scalar", c_v, abl_last_user)
            a_ready = 0
            for b in range(NB):
                u0 = 384 + 128 * (b - 3)
                sg = em.op("scalar", lambda En, b=b, u0=u0, h=h: En.activation(
                    abl[:, b * 512:(b + 1) * 512], mstx[:, u0:u0 + 512], AF.Sigmoid,
                    bias=nvcol[:, h:h + 1], scale=wcol[:, h:h + 1]), inc=c_s)
                em.w("vector", c_s, sg)
                a_ready = em.op("vector", lambda En, b=b, h=h: En.tensor_scalar_mul(
                    abl[:, b * 512:(b + 1) * 512], abl[:, b * 512:(b + 1) * 512],
                    c8col[:, h:h + 1]), inc=c_v)

            for seq in range(S):
                m = h * S + seq
                sl = m % 2
                em.w("tensor", c_ld, qkv_ready[m])
                if first_p2:
                    em.w("tensor", c_v, p1_done_v)
                    first_p2 = False
                cb = m % 2
                qg_last = {}
                v_rd_q = {}
                s_cx_q = {}
                for qg in range(4):
                    cxbank = CXB[qg]
                    nkt = 4 * qg + 4
                    binfo = []
                    for kt in range(nkt):
                        beta = 4 * qg - kt
                        bi = beta + 3
                        partial = beta <= 0
                        jj = blk_j
                        blk_j += 1
                        sb_sl = jj % 4
                        sbank = psA if jj % 2 == 0 else psB
                        em.w("tensor", c_v, stt_done.get(jj - 2, 0))
                        mm_s = em.op("tensor", lambda En, sbank=sbank, sl=sl, kt=kt, qg=qg: En.matmul(
                            sbank[:, :], ktb[:, sl * N + kt * 128: sl * N + (kt + 1) * 128],
                            qtb[:, sl * N + qg * 512: sl * N + (qg + 1) * 512],
                            start=True, stop=True), inc=c_t)
                        em.w("vector", c_t, mm_s)
                        em.w("vector", c_v, a_ready)
                        em.w("vector", c_s, exp_done.get(jj - 4, 0))
                        vst = em.op("vector", lambda En, sbank=sbank, sb_sl=sb_sl, bi=bi: En.scalar_tensor_tensor(
                            out=sbs[:, sb_sl * 512:(sb_sl + 1) * 512], in0=sbank[:, :],
                            scalar=0.0, in1=abl[:, bi * 512:(bi + 1) * 512],
                            op0=ALU.max, op1=ALU.mult), inc=c_v)
                        stt_done[jj] = vst
                        em.w("scalar", c_v, vst)
                        pf = pb_free.get(kt)
                        if pf is not None:
                            em.w("scalar", pf[0], pf[1])
                        sxp = em.op("scalar", lambda En, sb_sl=sb_sl, kt=kt: En.activation(
                            pb[:, kt * 512:(kt + 1) * 512], sbs[:, sb_sl * 512:(sb_sl + 1) * 512],
                            AF.Exp), inc=c_s)
                        exp_done[jj] = sxp
                        gsl = None
                        ps_m = None
                        if partial:
                            ps_m = (qg % 2) * 4 + (kt - 4 * qg)
                            em.w("gpsimd", c_s, sxp)
                            em.w("gpsimd", c_t, pm_free.get(ps_m, 0))
                            gsl = em.op("gpsimd", lambda En, kt=kt, ps_m=ps_m, beta=beta: En.affine_select(
                                pm[:, ps_m * 512:(ps_m + 1) * 512], pb[:, kt * 512:(kt + 1) * 512],
                                [[1, 512]], ALU.is_ge, 0.0, base=128 * beta,
                                channel_multiplier=-1), inc=c_g)
                        binfo.append((kt, partial, ps_m, sxp, gsl))
                    for (kt, partial, ps_m, sxp, gsl) in binfo:
                        if partial:
                            em.w("tensor", c_g, gsl)
                        else:
                            em.w("tensor", c_s, sxp)
                        if kt == 0:
                            em.w("tensor", c_v, bankfree_v[qg])
                            em.w("tensor", c_s, bankfree_s[qg])
                        if partial:
                            mm_c = em.op("tensor", lambda En, cxbank=cxbank, sl=sl, kt=kt, ps_m=ps_m, nkt=nkt: En.matmul(
                                cxbank[0:65, :], vxb[:, sl * NT * 65 + kt * 65: sl * NT * 65 + (kt + 1) * 65],
                                pm[:, ps_m * 512:(ps_m + 1) * 512],
                                start=(kt == 0), stop=(kt == nkt - 1)), inc=c_t)
                            pm_free[ps_m] = mm_c
                            pb_free[kt] = (c_g, gsl)
                        else:
                            mm_c = em.op("tensor", lambda En, cxbank=cxbank, sl=sl, kt=kt, nkt=nkt: En.matmul(
                                cxbank[0:65, :], vxb[:, sl * NT * 65 + kt * 65: sl * NT * 65 + (kt + 1) * 65],
                                pb[:, kt * 512:(kt + 1) * 512],
                                start=(kt == 0), stop=(kt == nkt - 1)), inc=c_t)
                            pb_free[kt] = (c_t, mm_c)
                    qg_last[qg] = mm_c
                    em.w("vector", c_t, mm_c)
                    em.w("vector", c_t, rd_free.get(qg, 0))
                    em.op("vector", lambda En, cxbank=cxbank, qg=qg: En.reciprocal(
                        rd[:, qg * 512:(qg + 1) * 512], cxbank[64:65, :]))
                    v_rd_q[qg] = em.drain("vector", inc=c_v)
                    em.w("scalar", c_t, mm_c)
                    em.w("scalar", c_v, sbctx_free.get(qg, 0))
                    s_cx_q[qg] = em.op("scalar", lambda En, cxbank=cxbank, qg=qg: En.copy(
                        sbctx[:, qg * 512:(qg + 1) * 512], cxbank[0:64, :]), inc=c_s)
                    bankfree_v[qg] = v_rd_q[qg]
                    bankfree_s[qg] = s_cx_q[qg]
                for qg in range(4):
                    half = qg % 2
                    em.w("tensor", c_v, v_rd_q[qg])
                    em.w("tensor", c_v, evac_half_free.get(half, 0))
                    t_bc = em.op("tensor", lambda En, qg=qg, half=half: En.matmul(
                        psE2[half * 64:(half + 1) * 64, :], ones_f[:, 0:64],
                        rd[:, qg * 512:(qg + 1) * 512], start=True, stop=True), inc=c_t)
                    rd_free[qg] = t_bc
                    em.w("vector", c_t, t_bc)
                    em.w("vector", c_s, s_cx_q[qg])
                    em.w("vector", c_st, ctxn_free[cb])
                    v_ev = em.op("vector", lambda En, cb=cb, qg=qg, half=half: En.tensor_tensor(
                        ctxn[:, cb * N + qg * 512: cb * N + (qg + 1) * 512],
                        sbctx[:, qg * 512:(qg + 1) * 512],
                        psE2[half * 64:(half + 1) * 64, :], op=ALU.mult), inc=c_v)
                    evac_half_free[half] = v_ev
                    sbctx_free[qg] = v_ev
                slot_free_t[sl] = qg_last[3]
                if m + 2 < M:
                    issue_qkv_load(m + 2)
                abl_last_user = stt_done[blk_j - 1]
                em.w("gpsimd", c_v, v_ev)
                ctxn_free[cb] = em.op("gpsimd", lambda En, seq=seq, h=h, cb=cb: En.dma_start(
                    out=dap(cts, seq * D * N + h * 64 * N, [[N, 64], [1, N]]),
                    in_=ctxn[0:64, cb * N:(cb + 1) * N]), inc=c_st, k=16)
            em.flush()
    p2_done_st = c_st.n
    p2_done_v = c_v.n
    p2_done_s = c_s.n

    with nc.sbuf_tensor("acc", [128, NT * 1024], F32) as acc, \
         nc.sbuf_tensor("y1t", [128, DT * N], BF16) as y1t, \
         nc.sbuf_tensor("arena", [128, 16384], BF16) as arena, \
         nc.sbuf_tensor("w1q", [128, 8192], BF16) as w1q, \
         nc.sbuf_tensor("w2q", [128, 8192], BF16) as w2q, \
         nc.sbuf_tensor("xres", [128, 2 * 1024], F32) as xres, \
         nc.sbuf_tensor("y1stg", [128, 1024], BF16) as y1stg, \
         nc.sbuf_tensor("logits", [128, NT * E], F32) as logits, \
         nc.sbuf_tensor("masks", [128, NT * E], F32) as masks, \
         nc.sbuf_tensor("mkt", [8, 128], F32) as mkt, \
         nc.sbuf_tensor("gwb", [128, D], F32) as gwb, \
         nc.sbuf_tensor("gsc", [128, D], F32) as gsc, \
         nc.sbuf_tensor("stats", [128, 8 * NT], F32) as stats, \
         nc.sbuf_tensor("cols", [128, 8], F32) as cols:
        SUM0, SUM1, SSQ0, SSQ1, MEAN, RSTD, TMP, NF = range(8)

        def stat(k, qt):
            return stats[:, k * NT + qt: k * NT + qt + 1]

        def statblk(k):
            return stats[:, k * NT:(k + 1) * NT]

        w1_ready = 0
        w2_ready = 0
        m1_done_prev = 0
        m2_done_prev = 0

        def load_w1(e, fq):
            nonlocal w1_ready
            em.w("sync", c_t, m1_done_prev)
            w1_ready = em.op("sync", lambda En, e=e, fq=fq: En.dma_start(
                out=w1q[:, :],
                in_=dap(w1b, e * D * DFF + fq * 1024, [[DFF, 128], [128 * DFF, DT], [1, 1024]])), inc=c_ld, k=16)

        def load_w2(e, fq):
            nonlocal w2_ready
            em.w("sync", c_t, m2_done_prev)
            w2_ready = em.op("sync", lambda En, e=e, fq=fq: En.dma_start(
                out=w2q[:, :],
                in_=dap(w2b, e * DFF * D + fq * 1024 * D, [[D, 128], [128 * D, 8], [1, 1024]])), inc=c_ld, k=16)

        load_w1(0, 0)
        load_w2(0, 0)

        arena_last_use = 0
        y1t_last_use = 0
        gwb_used = 0
        prev_out_last = 0
        for seq in range(S):
            em.w("sync", c_st, p2_done_st)
            if seq > 0:
                em.w("sync", c_t, arena_last_use)
            lctx = em.op("sync", lambda En, seq=seq: En.dma_start(
                out=arena[:, :],
                in_=dap(cts, seq * D * N, [[N, 128], [128 * N, DT], [1, N]])), inc=c_ld, k=16)
            xres_free = [0, 0]
            wo_evacs = {}
            mm = 0
            for qt in range(NT):
                xsl = qt % 2
                em.w("sync", c_v, xres_free[xsl])
                lxr = em.op("sync", lambda En, seq=seq, qt=qt, xsl=xsl: En.dma_start(
                    out=xres[:, xsl * 1024:(xsl + 1) * 1024],
                    in_=dap(xin, seq * N * D + qt * 128 * D, [[D, 128], [1, D]])), inc=c_ld, k=16)
                for ch in range(2):
                    bank = psA if ch == 0 else psB
                    em.w("tensor", c_ld, lctx)
                    em.w("tensor", c_ld, prepass_ld)
                    em.w("tensor", c_v, prepass_v)
                    em.w("tensor", c_v, p2_done_v)
                    em.w("tensor", c_s, p2_done_s)
                    if seq > 0 and qt == 0:
                        em.w("tensor", c_st, prev_out_last)
                    em.w("tensor", c_v, wo_evacs.get((qt - 1, ch), 0))
                    for dk in range(DT):
                        mm = em.op("tensor", lambda En, bank=bank, dk=dk, qt=qt, ch=ch: En.matmul(
                            bank[:, :], arena[:, dk * N + qt * 128: dk * N + (qt + 1) * 128],
                            wo_b[:, dk * D + ch * 512: dk * D + (ch + 1) * 512],
                            start=(dk == 0), stop=(dk == DT - 1)), inc=c_t)
                    em.w("vector", c_t, mm)
                    em.w("vector", c_ld, lxr)
                    if seq > 0 and qt == 0 and ch == 0:
                        em.w("vector", c_st, prev_out_last)
                    em.op("vector", lambda En, bank=bank, qt=qt, ch=ch: En.scalar_tensor_tensor(
                        out=acc[:, qt * 1024 + ch * 512: qt * 1024 + (ch + 1) * 512],
                        in0=bank[:, :], scalar=1.0, in1=bobc[:, ch * 512:(ch + 1) * 512],
                        op0=ALU.mult, op1=ALU.add), inc=c_v)
                    em.op("vector", lambda En, qt=qt, ch=ch, xsl=xsl: En.scalar_tensor_tensor(
                        out=acc[:, qt * 1024 + ch * 512: qt * 1024 + (ch + 1) * 512],
                        in0=acc[:, qt * 1024 + ch * 512: qt * 1024 + (ch + 1) * 512],
                        scalar=1.0, in1=xres[:, xsl * 1024 + ch * 512: xsl * 1024 + (ch + 1) * 512],
                        op0=ALU.mult, op1=ALU.add,
                        accum_out=stat(SUM0 + ch, qt)), inc=c_v)
                    ev = em.op("vector", lambda En, qt=qt, ch=ch: En.scalar_tensor_tensor(
                        out=gsc[:, ch * 512:(ch + 1) * 512],
                        in0=acc[:, qt * 1024 + ch * 512: qt * 1024 + (ch + 1) * 512],
                        scalar=1.0, in1=acc[:, qt * 1024 + ch * 512: qt * 1024 + (ch + 1) * 512],
                        op0=ALU.mult, op1=ALU.mult,
                        accum_out=stat(SSQ0 + ch, qt)), inc=c_v)
                    wo_evacs[(qt, ch)] = ev
                xres_free[xsl] = wo_evacs[(qt, 1)]
            em.w("vector", c_v, wo_evacs[(NT - 1, 1)])
            em.drain("vector")
            em.op("vector", lambda En: En.tensor_tensor(statblk(MEAN), statblk(SUM0), statblk(SUM1), op=ALU.add))
            em.drain("vector")
            em.op("vector", lambda En: En.tensor_scalar_mul(statblk(MEAN), statblk(MEAN), 1.0 / D))
            em.drain("vector")
            em.op("vector", lambda En: En.tensor_tensor(statblk(TMP), statblk(SSQ0), statblk(SSQ1), op=ALU.add))
            em.drain("vector")
            em.op("vector", lambda En: En.tensor_scalar_mul(statblk(TMP), statblk(TMP), 1.0 / D))
            em.drain("vector")
            em.op("vector", lambda En: En.tensor_tensor(statblk(RSTD), statblk(MEAN), statblk(MEAN), op=ALU.mult))
            em.drain("vector")
            em.op("vector", lambda En: En.tensor_tensor(statblk(TMP), statblk(TMP), statblk(RSTD), op=ALU.subtract))
            v_var = em.drain("vector", inc=c_v)
            em.w("scalar", c_v, v_var)
            s_sd = em.op("scalar", lambda En: En.activation(statblk(RSTD), statblk(TMP), AF.Sqrt, bias=epsc[:, 0:1]), inc=c_s)
            em.w("vector", c_s, s_sd)
            em.op("vector", lambda En: En.reciprocal(statblk(RSTD), statblk(RSTD)))
            v_rstd = em.drain("vector", inc=c_v)
            y1_tr = {}
            cst = 0
            for qt in range(NT):
                em.w("vector", c_v, v_rstd)
                nrm = em.op("vector", lambda En, qt=qt: En.tensor_scalar(
                    out=acc[:, qt * 1024:(qt + 1) * 1024], in0=acc[:, qt * 1024:(qt + 1) * 1024],
                    scalar1=stat(MEAN, qt), scalar2=stat(RSTD, qt),
                    op0=ALU.subtract, op1=ALU.mult), inc=c_v)
                em.w("scalar", c_v, nrm)
                em.w("scalar", c_v, y1_tr.get(qt - 1, 0))
                cst = em.op("scalar", lambda En, qt=qt: En.copy(y1stg[:, :], acc[:, qt * 1024:(qt + 1) * 1024]), inc=c_s)
                em.w("tensor", c_s, cst)
                if qt >= 1:
                    em.w("tensor", c_v, y1_tr[qt - 1])
                if seq > 0 and qt == 0:
                    em.w("tensor", c_t, y1t_last_use)
                ttr = 0
                for dt in range(DT):
                    ttr = em.op("tensor", lambda En, dt=dt: En.transpose(
                        psT[:, dt * 128:(dt + 1) * 128], y1stg[:, dt * 128:(dt + 1) * 128],
                        eye_b[:, :]), inc=c_t)
                em.w("vector", c_t, ttr)
                if seq > 0 and qt == 0:
                    em.w("vector", c_t, y1t_last_use)
                y1_tr[qt] = em.op("vector", lambda En, qt=qt: En.tensor_copy(
                    dap(y1t, qt * 128, [[DT * N, 128], [N, DT], [1, 128]]), psT[:, :]), inc=c_v)
            y1t_ready = y1_tr[NT - 1]
            lg = 0
            gw_row_reads = 0
            for e in range(E):
                em.w("sync", c_v, wo_evacs[(NT - 1, 1)])
                em.w("sync", c_t, gw_row_reads)
                lgw = em.op("sync", lambda En, e=e: En.dma_start(
                    out=xres[0:1, 0:1024], in_=dap(gw, e * D, [[D, 1], [1, D]])), inc=c_ld, k=16)
                em.w("tensor", c_ld, lgw)
                em.w("tensor", c_v, prepass_v)
                if gwb_used:
                    em.w("tensor", c_v, gwb_used)
                em.op("tensor", lambda En, e=e: En.matmul(
                    psE2[:, :], ones_f[:, :], xres[0:1, 0:512], start=True, stop=True), inc=c_t)
                t_g2 = em.op("tensor", lambda En, e=e: En.matmul(
                    psD[:, :], ones_f[:, :], xres[0:1, 512:1024],
                    start=True, stop=True), inc=c_t)
                gw_row_reads = t_g2
                em.w("vector", c_t, t_g2)
                em.op("vector", lambda En: En.tensor_copy(gwb[:, 0:512], psE2[:, 0:512]), inc=c_v)
                v_gw = em.op("vector", lambda En: En.tensor_copy(gwb[:, 512:1024], psD[:, 0:512]), inc=c_v)
                for qt in range(NT):
                    lg = em.op("vector", lambda En, qt=qt, e=e: En.scalar_tensor_tensor(
                        out=gsc[:, :], in0=acc[:, qt * 1024:(qt + 1) * 1024], scalar=1.0,
                        in1=gwb[:, :], op0=ALU.mult, op1=ALU.mult,
                        accum_out=logits[:, qt * E + e: qt * E + e + 1]), inc=c_v)
                gwb_used = lg
            logits_done = lg
            em.drain("vector")
            mask_v = {}
            for qt in range(NT):
                em.op("vector", lambda En, qt=qt: En.tensor_tensor(
                    logits[:, qt * E:(qt + 1) * E], logits[:, qt * E:(qt + 1) * E], gbb[:, :], op=ALU.add))
                em.drain("vector")
                em.op("vector", lambda En, qt=qt: En.reduce_max(cols[:, 0:1], logits[:, qt * E:(qt + 1) * E], axis=AX.X))
                em.drain("vector")
                em.op("vector", lambda En: En.memset(cols[:, 1:2], 1.0))
                em.drain("vector")
                for e in range(E):
                    em.op("vector", lambda En, qt=qt, e=e: En.tensor_tensor(
                        cols[:, 2:3], logits[:, qt * E + e: qt * E + e + 1], cols[:, 0:1], op=ALU.is_ge))
                    em.drain("vector")
                    em.op("vector", lambda En, qt=qt, e=e: En.tensor_tensor(
                        masks[:, qt * E + e: qt * E + e + 1], cols[:, 2:3], cols[:, 1:2], op=ALU.mult))
                    em.drain("vector")
                    em.op("vector", lambda En, qt=qt, e=e: En.tensor_tensor(
                        cols[:, 1:2], cols[:, 1:2], masks[:, qt * E + e: qt * E + e + 1], op=ALU.subtract))
                    em.drain("vector")
                mask_v[qt] = em.drain("vector", inc=c_v)
            b2_evac = 0
            b2add = 0
            mkt_reads = 0
            for qt in range(NT):
                em.w("tensor", c_v, mask_v[qt])
                em.w("tensor", c_v, b2_evac)
                tmk = em.op("tensor", lambda En, qt=qt: En.transpose(
                    psC[0:8, 0:128], masks[:, qt * E:(qt + 1) * E], eye_f[:, :]), inc=c_t)
                em.w("vector", c_t, tmk)
                em.w("vector", c_t, mkt_reads)
                b2_evac = em.op("vector", lambda En, qt=qt: En.tensor_copy(
                    mkt[:, :], psC[0:8, 0:128]), inc=c_v)
                for ch in range(2):
                    em.w("tensor", c_v, b2_evac)
                    if b2add:
                        em.w("tensor", c_v, b2add)
                    tb2 = em.op("tensor", lambda En, qt=qt, ch=ch: En.matmul(
                        psD[:, :], mkt[:, :], b2s[:, ch * 512:(ch + 1) * 512],
                        start=True, stop=True), inc=c_t)
                    mkt_reads = tb2
                    em.w("vector", c_t, tb2)
                    b2add = em.op("vector", lambda En, qt=qt, ch=ch: En.scalar_tensor_tensor(
                        out=acc[:, qt * 1024 + ch * 512: qt * 1024 + (ch + 1) * 512],
                        in0=psD[:, :], scalar=1.0,
                        in1=acc[:, qt * 1024 + ch * 512: qt * 1024 + (ch + 1) * 512],
                        op0=ALU.mult, op1=ALU.add), inc=c_v)
            acc_base = b2add
            em.flush()

            h_evac = {}
            acc_upd = {}
            for e in range(E):
                for fq in range(4):
                    w1_now, w2_now = w1_ready, w2_ready
                    m1_last = 0
                    for ft in range(8):
                        for tb in range(4):
                            idx = ft * 4 + tb
                            bank = psA if idx % 2 == 0 else psB
                            em.w("tensor", c_ld, w1_now)
                            em.w("tensor", c_v, y1t_ready)
                            if e == 0 and fq == 0:
                                em.w("tensor", c_v, acc_base)
                            em.w("tensor", c_s, h_evac.get(idx - 2, 0))
                            for dk in range(DT):
                                mmx = em.op("tensor", lambda En, bank=bank, ft=ft, tb=tb, dk=dk: En.matmul(
                                    bank[:, :], w1q[:, dk * 1024 + ft * 128: dk * 1024 + (ft + 1) * 128],
                                    y1t[:, dk * N + tb * 512: dk * N + (tb + 1) * 512],
                                    start=(dk == 0), stop=(dk == DT - 1)), inc=c_t)
                            m1_last = mmx
                            em.w("scalar", c_t, mmx)
                            em.w("scalar", c_t, m2_done_prev if idx == 0 else 0)
                            h_evac[idx] = em.op("scalar", lambda En, bank=bank, ft=ft, tb=tb, e=e, fq=fq: En.activation(
                                arena[:, ft * N + tb * 512: ft * N + (tb + 1) * 512], bank[:, :],
                                AF.Relu, bias=b1s[:, (fq * 8 + ft) * E + e: (fq * 8 + ft) * E + e + 1]), inc=c_s)
                    m1_done_prev = m1_last
                    h_ready = h_evac[31]
                    nxt = e * 4 + fq + 1
                    if nxt < 32:
                        load_w1(nxt // 4, nxt % 4)
                    elif seq < S - 1:
                        load_w1(0, 0)
                    m2_last = 0
                    for tt in range(NT):
                        for ch in range(2):
                            idx = tt * 2 + ch
                            bank = psC if idx % 2 == 0 else psD
                            em.w("tensor", c_ld, w2_now)
                            em.w("tensor", c_s, h_ready)
                            em.w("tensor", c_v, acc_upd.get(idx - 2, 0))
                            for ft in range(8):
                                mmx = em.op("tensor", lambda En, bank=bank, tt=tt, ch=ch, ft=ft: En.matmul(
                                    bank[:, :], arena[:, ft * N + tt * 128: ft * N + (tt + 1) * 128],
                                    w2q[:, ft * 1024 + ch * 512: ft * 1024 + (ch + 1) * 512],
                                    start=(ft == 0), stop=(ft == 7)), inc=c_t)
                            m2_last = mmx
                            em.w("vector", c_t, mmx)
                            em.w("vector", c_v, acc_base)
                            acc_upd[idx] = em.op("vector", lambda En, bank=bank, tt=tt, ch=ch, e=e: En.scalar_tensor_tensor(
                                out=acc[:, tt * 1024 + ch * 512: tt * 1024 + (ch + 1) * 512],
                                in0=bank[:, :], scalar=masks[:, tt * E + e: tt * E + e + 1],
                                in1=acc[:, tt * 1024 + ch * 512: tt * 1024 + (ch + 1) * 512],
                                op0=ALU.mult, op1=ALU.add), inc=c_v)
                    m2_done_prev = m2_last
                    if nxt < 32:
                        load_w2(nxt // 4, nxt % 4)
                    elif seq < S - 1:
                        load_w2(0, 0)
                em.flush()
            moe_done_v = acc_upd[NT * 2 - 1]
            y1t_last_use = m1_done_prev
            arena_last_use = m2_done_prev

            for qt in range(NT):
                em.w("vector", c_v, moe_done_v)
                em.op("vector", lambda En, qt=qt: En.tensor_scalar(
                    out=gsc[:, :], in0=acc[:, qt * 1024:(qt + 1) * 1024], scalar1=1.0, scalar2=0.0,
                    op0=ALU.mult, op1=ALU.add, accum_out=stat(SUM0, qt)), inc=c_v)
                em.op("vector", lambda En, qt=qt: En.scalar_tensor_tensor(
                    out=gsc[:, :], in0=acc[:, qt * 1024:(qt + 1) * 1024], scalar=1.0,
                    in1=acc[:, qt * 1024:(qt + 1) * 1024], op0=ALU.mult, op1=ALU.mult,
                    accum_out=stat(SSQ0, qt)), inc=c_v)
            em.drain("vector")
            em.op("vector", lambda En: En.tensor_scalar_mul(statblk(MEAN), statblk(SUM0), 1.0 / D))
            em.drain("vector")
            em.op("vector", lambda En: En.tensor_scalar_mul(statblk(TMP), statblk(SSQ0), 1.0 / D))
            em.drain("vector")
            em.op("vector", lambda En: En.tensor_tensor(statblk(RSTD), statblk(MEAN), statblk(MEAN), op=ALU.mult))
            em.drain("vector")
            em.op("vector", lambda En: En.tensor_tensor(statblk(TMP), statblk(TMP), statblk(RSTD), op=ALU.subtract))
            v_var2 = em.drain("vector", inc=c_v)
            em.w("scalar", c_v, v_var2)
            s_sd2 = em.op("scalar", lambda En: En.activation(statblk(RSTD), statblk(TMP), AF.Sqrt, bias=epsc[:, 0:1]), inc=c_s)
            em.w("vector", c_s, s_sd2)
            em.op("vector", lambda En: En.reciprocal(statblk(RSTD), statblk(RSTD)))
            v_rstd2 = em.drain("vector", inc=c_v)
            for qt in range(NT):
                em.w("vector", c_v, v_rstd2)
                nout = em.op("vector", lambda En, qt=qt: En.tensor_scalar(
                    out=acc[:, qt * 1024:(qt + 1) * 1024], in0=acc[:, qt * 1024:(qt + 1) * 1024],
                    scalar1=stat(MEAN, qt), scalar2=stat(RSTD, qt),
                    op0=ALU.subtract, op1=ALU.mult), inc=c_v)
                em.w("gpsimd", c_v, nout)
                prev_out_last = em.op("gpsimd", lambda En, seq=seq, qt=qt: En.dma_start(
                    out=dap(yout, seq * N * D + qt * 128 * D, [[D, 128], [1, D]]),
                    in_=acc[:, qt * 1024:(qt + 1) * 1024]), inc=c_st, k=16)
            em.flush()

        em.w("gpsimd", c_st, c_st.n)
        em.flush()
    es.close()
    return nc


_CACHE = {}


def kernel(L0_attn_x, L0_attn_Wq, L0_attn_Wk, L0_attn_Wv, L0_attn_Wo, L0_attn_bo,
           L0_attn_w, L0_attn_v, L0_ffn_gate_w, L0_ffn_gate_b, L0_ffn_expert_bias,
           L0_ffn_w1, L0_ffn_b1, L0_ffn_w2, L0_ffn_b2):
    if "nc" not in _CACHE:
        _CACHE["nc"] = build()
    nc = _CACHE["nc"]

    def f32(a):
        return np.ascontiguousarray(np.asarray(a, dtype=np.float32))

    x = f32(L0_attn_x)
    b1 = f32(L0_ffn_b1)
    b1rr = np.ascontiguousarray(b1.T.reshape(FT, 128, E).transpose(1, 0, 2).reshape(128, FT * E))
    common = {
        "wq": f32(L0_attn_Wq), "wk": f32(L0_attn_Wk), "wv": f32(L0_attn_Wv),
        "wo": f32(L0_attn_Wo), "bo": f32(L0_attn_bo).reshape(1, D),
        "wdec": f32(L0_attn_w).reshape(1, H), "vdec": f32(L0_attn_v).reshape(1, H),
        "gw": f32(L0_ffn_gate_w), "gb": f32(L0_ffn_gate_b).reshape(1, E),
        "ebias": f32(L0_ffn_expert_bias).reshape(1, E),
        "w1": f32(L0_ffn_w1), "b1r": b1rr, "w2": f32(L0_ffn_w2), "b2": f32(L0_ffn_b2),
        "eye": np.eye(128, dtype=np.float32),
        "onesr": np.ones((1, 128), dtype=np.float32),
    }
    in_maps = []
    for c in range(8):
        m = dict(common)
        m["x"] = np.ascontiguousarray(x[c * S:(c + 1) * S])
        in_maps.append(m)
    res = run_bass_kernel_spmd(nc, in_maps, core_ids=list(range(8)))
    out = np.concatenate([np.asarray(r["y"], dtype=np.float32) for r in res.results], axis=0)
    return out.reshape(32, N, D)
